# version 07
# baseline (speedup 1.0000x reference)
import math
import contextlib
import numpy as np
import ml_dtypes
import concourse.bass as bass
import concourse.mybir as mybir
from concourse.bass_utils import run_bass_kernel_spmd

F32 = mybir.dt.float32
BF16 = mybir.dt.bfloat16
AF = mybir.ActivationFunctionType
ALU = mybir.AluOpType
AX = mybir.AxisListType

B, S, E = 2, 8192, 2048
SJ = 512
TQ = 2048
NE = 8
LAMBDA_INIT = 0.8 - 0.6 * float(np.exp(-0.3 * 12))
SCALE = 1.0 / math.sqrt(128.0)
EPS = 1e-5
NT = 16

DMA_SEMS = (
    "sxc0", "sxc1", "sxc2", "sxc3", "sconst", "swqk", "swv", "swo", "smask",
    "sqw0", "sqw1", "sqw2", "sqw3", "sxt0", "sxt1", "sqt0", "sqt1",
    "sab0", "sab1", "sab2", "sab3", "sat0", "sat1", "srs0", "srs1",
    "sqt2", "sqt3",
    "s2a0", "s2a1", "sx2w0", "sx2w1", "sx2t", "swg0", "swg1", "swu0", "swu1",
    "swd", "srw", "sout0", "sout1", "sx2rb",
    "sxc4", "sxc5", "sxc6", "sxc7", "sxr0", "sxr1",
)
SEMS = DMA_SEMS + ("st", "sv", "ssc",
                   "cc0", "cc1", "cc2", "cc3", "cc4", "cc5", "cc6", "cc7")


class Pass:
    def __init__(self, which, eng, sems):
        self.which = which
        self.eng = eng
        self.sems = sems
        self.n = {s: 0 for s in SEMS}

    def _w(self, waits):
        seen = {}
        for s, v in waits:
            if v > 0:
                seen[s] = max(seen.get(s, 0), v)
        for s, v in seen.items():
            self.eng.wait_ge(self.sems[s], v)

    def op(self, eng_name, fn, inc=None, waits=()):
        if eng_name == self.which:
            self._w(waits)
            ins = fn(self.eng)
            if inc is not None and ins is not None:
                ins.then_inc(self.sems[inc[0]], inc[1])
        if inc is not None:
            self.n[inc[0]] += inc[1]

    def ten(self, fn, inc=False, waits=()):
        self.op("tensor", fn, inc=("st", 1) if inc else None, waits=waits)

    def vec(self, fn, waits=()):
        self.op("vector", fn, inc=("sv", 1),
                waits=[("sv", self.n["sv"])] + list(waits))

    def sca(self, fn, waits=()):
        self.op("scalar", fn, inc=("ssc", 1),
                waits=[("ssc", self.n["ssc"])] + list(waits))

    def gps(self, fn, sem, waits=()):
        self.op("gpsimd", fn, inc=(sem, 16), waits=waits)

    def syn(self, fn, sem, waits=()):
        self.op("sync", fn, inc=(sem, 16), waits=waits)

    def sqd(self, fn, sem, waits=()):
        self.op("scalar", fn, inc=(sem, 16), waits=waits)

    def wait(self, eng_name, waits):
        if eng_name == self.which:
            self._w(waits)


def build(debug=()):
    nc = bass.Bass("TRN2", target_bir_lowering=False)

    def din(name, shape, dtype=F32):
        return nc.dram_tensor(name, shape, dtype, kind="ExternalInput").ap()

    def dout(name, shape, dtype=F32):
        return nc.dram_tensor(name, shape, dtype, kind="ExternalOutput").ap()

    def dint(name, shape, dtype=BF16):
        return nc.dram_tensor(name, shape, dtype, kind="Internal").ap()

    ns = {}
    ns["debug"] = set(debug)
    ns["CCV"] = 16 if "fakecc" in set(debug) else 1
    ns["x_full"] = din("x_full", [S, E])
    ns["x_res"] = din("x_res", [TQ, E])
    for nm in ("wq", "wk", "wv"):
        ns[nm] = din(nm, [E, SJ])
    ns["wo_rows"] = din("wo_rows", [SJ, E])
    for nm in ("lam_q1", "lam_k1", "lam_q2", "lam_k2"):
        ns[nm] = din(nm, [128, 1])
    ns["subw"] = din("subw", [128, 256])
    ns["masks"] = din("masks", [128, 4 * 512], BF16)
    ns["ones_col"] = din("ones_col", [128, 1], BF16)
    ns["ones_row"] = din("ones_row", [1, 128], BF16)
    ns["router_w"] = din("router_w", [E, NE])
    ns["w_gate"] = din("w_gate", [NE, E, 1024])
    ns["w_up"] = din("w_up", [NE, E, 1024])
    ns["w_down"] = din("w_down", [NE, 1024, E])
    ns["sw_gate"] = din("sw_gate", [E, 2048])
    ns["sw_up"] = din("sw_up", [E, 2048])
    ns["sw_down"] = din("sw_down", [2048, E])
    ns["out"] = dout("out", [TQ, E])

    ns["xbf"] = dint("xbf", [S, E])
    ns["qts"] = dint("qts", [4 * 128, S])
    ns["a_nat"] = dint("a_nat", [S, SJ])
    ns["rs_in"] = dint("rs_in", [S, E])
    ns["rs_out"] = [dint(f"rs_out{k}", [TQ // 4, E]) for k in range(4)]
    ns["x2s"] = dint("x2s", [TQ, E])

    dbg = {}
    if "phase1" in ns["debug"]:
        dbg["d_kt"] = dout("d_kt", [4 * 128, S], BF16)
        dbg["d_v"] = dout("d_v", [2 * 128, 64 * 257], BF16)
        dbg["d_negm"] = dout("d_negm", [128, 1], F32)
        dbg["d_neglam"] = dout("d_neglam", [128, 1], F32)
        dbg["d_qts"] = dout("d_qts", [4 * 128, S], BF16)
    if "attn" in ns["debug"]:
        dbg["d_anat"] = dout("d_anat", [S, SJ], BF16)
        dbg["d_rsin"] = dout("d_rsin", [S, E], BF16)
    if "x2" in ns["debug"]:
        dbg["d_x2"] = dout("d_x2", [TQ, E], BF16)
    ns["dbg"] = dbg

    with contextlib.ExitStack() as ctx:
        E_ = ctx.enter_context
        sems = {s: E_(nc.semaphore(s)) for s in SEMS}
        ns["sems"] = sems
        block = E_(nc.Block())

        ns["smalls"] = E_(nc.sbuf_tensor("smalls", [128, 64], F32))
        ns["tiny"] = E_(nc.sbuf_tensor("tiny", [1, 16], F32))
        ns["tinyb"] = E_(nc.sbuf_tensor("tinyb", [1, 16], BF16))
        ns["onescol_sb"] = E_(nc.sbuf_tensor("onescol_sb", [128, 1], BF16))
        ns["onesrow_sb"] = E_(nc.sbuf_tensor("onesrow_sb", [1, 128], BF16))
        ns["subw_sb"] = E_(nc.sbuf_tensor("subw_sb", [128, 256], F32))
        for nm in ("lam_q1", "lam_k1", "lam_q2", "lam_k2"):
            ns[nm + "_sb"] = E_(nc.sbuf_tensor(nm + "_sb", [128, 1], BF16))
        ns["strip"] = E_(nc.sbuf_tensor("strip", [1, 1024], F32))
        ns["PB"] = [E_(nc.psum_tensor(f"PB{i}", [128, 1024], F32)) for i in range(4)]

        ab_scope = contextlib.ExitStack()
        ns["kslab"] = [ab_scope.enter_context(nc.sbuf_tensor(f"kslab{i}", [128, S], BF16)) for i in range(4)]
        ns["vbuf"] = [ab_scope.enter_context(nc.sbuf_tensor(f"vbuf{h}", [128, 64 * 257], BF16)) for h in range(2)]
        a_scope = contextlib.ExitStack()
        ns["wq_sb"] = a_scope.enter_context(nc.sbuf_tensor("wq_sb", [128, 16 * SJ], BF16))
        ns["wk_sb"] = a_scope.enter_context(nc.sbuf_tensor("wk_sb", [128, 16 * SJ], BF16))
        ns["xt"] = [a_scope.enter_context(nc.sbuf_tensor(f"xt{i}", [128, 16 * 512], BF16)) for i in range(2)]
        ns["qstage"] = [a_scope.enter_context(nc.sbuf_tensor(f"qstage{i}", [128, 512], BF16)) for i in range(2)]
        ns["qsq"] = [a_scope.enter_context(nc.sbuf_tensor(f"qsq{i}", [128, 512], BF16)) for i in range(2)]
        ns["a_scope"] = a_scope
        ns["ab_scope"] = ab_scope
        ns["nc"] = nc
        ns["alloc_done"] = set()

        counters = []

        def run_pass(which, eng):
            p = Pass(which, eng, sems)
            emit_all(p, ns)
            counters.append((which, dict(p.n)))

        @block.tensor
        def _(t):
            run_pass("tensor", t)

        @block.vector
        def _(v):
            run_pass("vector", v)

        @block.scalar
        def _(sc):
            run_pass("scalar", sc)

        @block.sync
        def _(s):
            run_pass("sync", s)

        @block.gpsimd
        def _(g):
            run_pass("gpsimd", g)

        ref = counters[0][1]
        for which, n in counters[1:]:
            assert n == ref, f"counter mismatch {which}: {n} vs {ref}"
        for key in ("s2b", "s2a", "a_scope", "ab_scope"):
            sc = ns.get(key)
            if sc is not None and key not in ns.get("closed", set()):
                try:
                    sc.close()
                except Exception:
                    pass
    return nc


def alloc_once(ns, key, fn):
    if key not in ns["alloc_done"]:
        ns["alloc_done"].add(key)
        fn()



def emit_all(p, ns):
    emit_preload(p, ns)
    emit_phaseA(p, ns)
    if "stopA" in ns["debug"]:
        emit_dbg_phase1(p, ns)
        emit_end(p, ns)
        return
    emit_phaseB(p, ns)
    if "phase1" in ns["debug"]:
        emit_dbg_phase1(p, ns)
    if "attn" in ns["debug"]:
        emit_dbg_attn(p, ns)
    if "stopB" in ns["debug"]:
        emit_end(p, ns)
        return
    emit_phase2a(p, ns)
    if "x2" in ns["debug"]:
        emit_dbg_x2(p, ns)
    if "stopx2" in ns["debug"]:
        emit_end(p, ns)
        return
    emit_phase2b(p, ns)
    emit_end(p, ns)


def emit_preload(p, ns):
    PB = ns["PB"]
    tiny, tinyb = ns["tiny"], ns["tinyb"]
    for part in range(8):
        p.gps(lambda g, part=part: g.dma_start(
            out=ns["xbf"][part * 1024:(part + 1) * 1024, :],
            in_=ns["x_full"][part * 1024:(part + 1) * 1024, :]), f"sxc{part}")
    p.gps(lambda g: g.dma_start(out=ns["onescol_sb"][:, :], in_=ns["ones_col"]), "sconst")
    p.gps(lambda g: g.dma_start(out=ns["onesrow_sb"][:, :], in_=ns["ones_row"]), "sconst")
    p.gps(lambda g: g.dma_start(out=ns["subw_sb"][:, :], in_=ns["subw"]), "sconst")
    for nm in ("lam_q1", "lam_k1", "lam_q2", "lam_k2"):
        p.gps(lambda g, nm=nm: g.dma_start(out=ns[nm + "_sb"][:, :], in_=ns[nm]), "sconst")
    NCONST = p.n["sconst"]
    ns["NCONST"] = NCONST

    for w_ap, sb in ((ns["wq"], ns["wq_sb"]), (ns["wk"], ns["wk_sb"])):
        p.gps(lambda g, w_ap=w_ap, sb=sb: g.dma_start(
            out=sb[:, :].rearrange("p (dd c) -> p dd c", c=512),
            in_=w_ap.rearrange("(dd p) c -> p dd c", p=128)), "swqk")

    p.vec(lambda v: v.memset(ns["smalls"][:, 32:33], EPS))
    p.vec(lambda v: v.tensor_scalar_mul(ns["subw_sb"][:, :], ns["subw_sb"][:, :],
                                        1.0 - LAMBDA_INIT),
          waits=[("sconst", NCONST)])

    p.ten(lambda t: t.matmul(PB[0][0:1, 0:1], ns["lam_q1_sb"][:, :],
                             ns["lam_k1_sb"][:, :], start=True, stop=True),
          waits=[("sconst", NCONST)])
    p.ten(lambda t: t.matmul(PB[1][0:1, 0:1], ns["lam_q2_sb"][:, :],
                             ns["lam_k2_sb"][:, :], start=True, stop=True), inc=True)
    st_dots = p.n["st"]
    p.sca(lambda sc: sc.activation(tiny[0:1, 3:4], PB[0][0:1, 0:1], AF.Exp),
          waits=[("st", st_dots)])
    p.sca(lambda sc: sc.activation(tiny[0:1, 4:5], PB[1][0:1, 0:1], AF.Exp))
    ssc_l = p.n["ssc"]
    p.vec(lambda v: v.tensor_sub(tiny[0:1, 2:3], tiny[0:1, 3:4], tiny[0:1, 4:5]),
          waits=[("ssc", ssc_l)])
    p.vec(lambda v: v.tensor_scalar(tiny[0:1, 2:3], tiny[0:1, 2:3], LAMBDA_INIT, -1.0,
                                    ALU.add, ALU.mult))
    p.vec(lambda v: v.tensor_copy(tinyb[0:1, 2:3], tiny[0:1, 2:3]))
    sv_l = p.n["sv"]
    p.ten(lambda t: t.matmul(PB[0][:, 0:1], ns["onesrow_sb"][0:1, :],
                             tinyb[0:1, 2:3], start=True, stop=True),
          inc=True, waits=[("sv", sv_l)])
    st_l = p.n["st"]
    p.vec(lambda v: v.tensor_copy(ns["smalls"][:, 1:2], PB[0][:, 0:1]),
          waits=[("st", st_l)])
    ns["sv_preload"] = p.n["sv"]


def emit_phaseA(p, ns):
    PB = ns["PB"]
    xt, qstage, qsq = ns["xt"], ns["qstage"], ns["qsq"]
    kslab, vbuf, strip = ns["kslab"], ns["vbuf"], ns["strip"]
    tiny, tinyb = ns["tiny"], ns["tinyb"]

    p.vec(lambda v: v.memset(strip[0:1, :], 0.0))

    st_tile_end = {-2: 0, -1: 0}
    sv_tile_end = {-1: ns["sv_preload"] + 1}
    st_strip = {-2: 0, -1: 0}
    strip_guard = [0, 0]
    nxt = [0, 0]

    for tt in range(NT):
        part = (tt * 512) // 1024
        for dd in range(16):
            w = []
            if dd == 0:
                w = [(f"sxc{part}", 16), ("st", st_tile_end[tt - 2])]
            p.syn(lambda s, tt=tt, dd=dd: s.dma_start(
                out=xt[tt % 2][:, 512 * dd:512 * dd + 512],
                in_=ns["xbf"][tt * 512:(tt + 1) * 512, 128 * dd:128 * dd + 128],
                transpose=True), f"sxt{tt % 2}", waits=w)
        nxt[tt % 2] += 16
        w_xt = (f"sxt{tt % 2}", 16 * nxt[tt % 2])
        xtile = xt[tt % 2]

        for grp in range(2):
            w_sb = (ns["wq_sb"], ns["wk_sb"])[grp]
            pbs = (PB[0], PB[1]) if grp == 0 else (PB[2], PB[3])
            for qc in range(4):
                pb = pbs[qc // 2]
                s0 = 512 * (qc % 2)
                for dd in range(16):
                    w = []
                    if dd == 0 and qc == 0:
                        w = [w_xt, ("swqk", 32),
                             ("sv", sv_tile_end[tt - 1])]
                    p.ten(lambda t, pb=pb, s0=s0, w_sb=w_sb, qc=qc, dd=dd, xtile=xtile:
                          t.matmul(pb[:, s0:s0 + 512],
                                   w_sb[:, 512 * dd + 128 * qc:512 * dd + 128 * qc + 128],
                                   xtile[:, 512 * dd:512 * dd + 512],
                                   start=(dd == 0), stop=(dd == 15)),
                          inc=(dd == 15), waits=w)
            st_grp = p.n["st"]
            for qc in range(4):
                pb = pbs[qc // 2]
                s0 = 512 * (qc % 2)
                if grp == 0:
                    if qc < 2:
                        wprev = (f"sqw{qc + 2}", 16 * tt)
                    else:
                        wprev = (f"sqw{qc - 2}", 16 * (tt + 1))
                    p.vec(lambda v, qc=qc, pb=pb, s0=s0: v.tensor_copy(
                        qstage[qc % 2][:, :], pb[:, s0:s0 + 512]),
                        waits=[("st", st_grp), wprev])
                    sv_ev = p.n["sv"]
                    p.sqd(lambda sc, qc=qc, tt=tt: sc.dma_start(
                        out=ns["qts"][128 * qc:128 * qc + 128, 512 * tt:512 * tt + 512],
                        in_=qstage[qc % 2][:, :]), f"sqw{qc}",
                        waits=[("sv", sv_ev)])
                else:
                    p.vec(lambda v, qc=qc, pb=pb, s0=s0, tt=tt: v.tensor_copy(
                        kslab[qc][:, 512 * tt:512 * tt + 512], pb[:, s0:s0 + 512]),
                        waits=[("st", st_grp)])
            for qc in range(4):
                if grp == 0:
                    p.vec(lambda v, qc=qc: v.tensor_mul(
                        qsq[qc % 2][:, :], qstage[qc % 2][:, :], qstage[qc % 2][:, :]),
                        waits=[("st", strip_guard[qc % 2])])
                else:
                    p.vec(lambda v, qc=qc, tt=tt: v.tensor_mul(
                        qsq[qc % 2][:, :],
                        kslab[qc][:, 512 * tt:512 * tt + 512],
                        kslab[qc][:, 512 * tt:512 * tt + 512]),
                        waits=[("st", strip_guard[qc % 2])])
                sv_sq = p.n["sv"]
                s0 = 0 if grp == 0 else 512
                p.ten(lambda t, qc=qc, s0=s0: t.matmul(
                    PB[0][0:1, s0:s0 + 512], ns["onescol_sb"][:, :],
                    qsq[qc % 2][:, :], start=(qc == 0), stop=(qc == 3)),
                    inc=True, waits=[("sv", sv_sq)])
                strip_guard[qc % 2] = p.n["st"]
            if grp == 1:
                st_strip[tt] = p.n["st"]
        p.vec(lambda v: v.tensor_max(strip[0:1, 0:512], strip[0:1, 0:512],
                                     PB[0][0:1, 0:512]),
              waits=[("st", st_strip[tt])])
        p.vec(lambda v: v.tensor_max(strip[0:1, 512:1024], strip[0:1, 512:1024],
                                     PB[0][0:1, 512:1024]))
        st_tile_end[tt] = p.n["st"]
        sv_tile_end[tt] = p.n["sv"]

    p.vec(lambda v: v.reduce_max(tiny[0:1, 5:6], strip[0:1, 0:512], AX.X))
    p.vec(lambda v: v.reduce_max(tiny[0:1, 6:7], strip[0:1, 512:1024], AX.X))
    p.vec(lambda v: v.tensor_mul(tiny[0:1, 7:8], tiny[0:1, 5:6], tiny[0:1, 6:7]))
    sv_m = p.n["sv"]
    p.sca(lambda sc: sc.activation(tiny[0:1, 8:9], tiny[0:1, 7:8], AF.Ln),
          waits=[("sv", sv_m)])
    p.sca(lambda sc: sc.activation(tiny[0:1, 9:10], tiny[0:1, 8:9], AF.Exp, scale=0.5))
    ssc_m = p.n["ssc"]
    p.vec(lambda v: v.tensor_scalar_mul(tiny[0:1, 10:11], tiny[0:1, 9:10], -SCALE),
          waits=[("ssc", ssc_m)])
    p.vec(lambda v: v.tensor_copy(tinyb[0:1, 10:11], tiny[0:1, 10:11]))
    sv_m2 = p.n["sv"]
    p.ten(lambda t: t.matmul(PB[1][:, 0:1], ns["onesrow_sb"][0:1, :],
                             tinyb[0:1, 10:11], start=True, stop=True),
          inc=True, waits=[("sv", sv_m2)])
    st_m = p.n["st"]
    p.vec(lambda v: v.tensor_copy(ns["smalls"][:, 0:1], PB[1][:, 0:1]),
          waits=[("st", st_m)])
    ns["sv_negm"] = p.n["sv"]

    wv_sb = ns["wq_sb"]
    p.gps(lambda g: g.dma_start(
        out=wv_sb[:, :].rearrange("p (dd c) -> p dd c", c=512),
        in_=ns["wv"].rearrange("(dd p) c -> p dd c", p=128)), "swv",
        waits=[("st", st_tile_end[NT - 1])])

    for h in range(2):
        p.vec(lambda v, h=h: v.memset(
            vbuf[h][:, :].rearrange("p (t c) -> p t c", c=257)[:, :, 256], 1.0))
    sv_ones = p.n["sv"]

    st_vtile = {-2: st_tile_end[NT - 1], -1: st_tile_end[NT - 1]}
    sv_vtile = {-1: sv_ones}
    for tt in range(NT):
        for dd in range(16):
            w = []
            if dd == 0:
                w = [("st", st_vtile[tt - 2])]
            p.syn(lambda s, tt=tt, dd=dd: s.dma_start(
                out=xt[tt % 2][:, 512 * dd:512 * dd + 512],
                in_=ns["xbf"][tt * 512:(tt + 1) * 512, 128 * dd:128 * dd + 128],
                transpose=True), f"sxt{tt % 2}", waits=w)
        nxt[tt % 2] += 16
        w_xt = (f"sxt{tt % 2}", 16 * nxt[tt % 2])
        xtile = xt[tt % 2]
        for i in range(4):
            pb = (PB[0], PB[1])[i // 2]
            s0 = 512 * (i % 2)
            for dd in range(16):
                w = []
                if i == 0 and dd == 0:
                    w = [w_xt, ("swv", 16), ("sv", sv_vtile[tt - 1])]
                p.ten(lambda t, pb=pb, s0=s0, i=i, dd=dd, xtile=xtile:
                      t.matmul(pb[:, s0:s0 + 512],
                               xtile[:, 512 * dd + 128 * i:512 * dd + 128 * i + 128],
                               wv_sb[:, 512 * dd:512 * dd + 512],
                               start=(dd == 0), stop=(dd == 15)),
                      inc=(dd == 15), waits=w)
        st_grp = p.n["st"]
        for i in range(4):
            pb = (PB[0], PB[1])[i // 2]
            s0 = 512 * (i % 2)
            kk = 4 * tt + i
            for h in range(2):
                p.vec(lambda v, pb=pb, s0=s0, h=h, kk=kk: v.tensor_copy(
                    vbuf[h][:, 257 * kk:257 * kk + 256],
                    pb[:, s0 + 256 * h:s0 + 256 * h + 256]),
                    waits=[("st", st_grp)])
        st_vtile[tt] = st_grp
        sv_vtile[tt] = p.n["sv"]
    ns["st_A_end"] = p.n["st"]
    ns["sv_A_end"] = p.n["sv"]


def emit_phaseB(p, ns):
    nc = ns["nc"]
    PB = ns["PB"]
    kslab, vbuf = ns["kslab"], ns["vbuf"]
    c = ns["smalls"]

    def allocB():
        ns["a_scope"].close()
        ns.setdefault("closed", set()).add("a_scope")
        sc_ = ns["ab_scope"].enter_context
        ns["qt_buf"] = [sc_(nc.sbuf_tensor(f"qt{i}", [128, 512], BF16)) for i in range(4)]
        ns["pt_buf"] = [sc_(nc.sbuf_tensor(f"pt{i}", [128, 1024], BF16)) for i in range(2)]
        ns["o1ev"] = sc_(nc.sbuf_tensor("o1ev", [128, 1028], F32))
        ns["o2ev"] = sc_(nc.sbuf_tensor("o2ev", [128, 1028], F32))
        ns["acomb"] = sc_(nc.sbuf_tensor("acomb", [128, 256], F32))
        ns["ajunk"] = sc_(nc.sbuf_tensor("ajunk", [128, 256], F32))
        ns["a_unit"] = [sc_(nc.sbuf_tensor(f"aun{i}", [128, 1024], BF16)) for i in range(2)]
        ns["at_sb"] = [sc_(nc.sbuf_tensor(f"at{i}", [128, 4096], BF16)) for i in range(2)]
        ns["wo_sb"] = sc_(nc.sbuf_tensor("wo_sb", [128, 4 * E], BF16))
        ns["wo_stage"] = [sc_(nc.sbuf_tensor(f"wost{i}", [128, E], BF16)) for i in range(2)]
        ns["masks_sb"] = sc_(nc.sbuf_tensor("masks_sb", [128, 4 * 512], BF16))
    alloc_once(ns, "B", allocB)
    qt_buf, pt_buf = ns["qt_buf"], ns["pt_buf"]
    o1ev, acomb, ajunk = ns["o1ev"], ns["acomb"], ns["ajunk"]
    o2ev = ns["o2ev"]
    a_unit, at_sb = ns["a_unit"], ns["at_sb"]
    wo_sb, wo_stage = ns["wo_sb"], ns["wo_stage"]

    QW_TOT = [(f"sqw{i}", 16 * NT) for i in range(4)]
    w_space = [("st", ns["st_A_end"]), ("sv", ns["sv_A_end"])] + QW_TOT
    p.gps(lambda g: g.dma_start(
        out=wo_sb[:, :].rearrange("p (ca cc) -> p ca cc", cc=E),
        in_=ns["wo_rows"].rearrange("(ca pp) cc -> pp ca cc", pp=128)), "swo",
        waits=w_space)
    p.gps(lambda g: g.dma_start(out=ns["masks_sb"][:, :], in_=ns["masks"]),
          "smask", waits=w_space)

    st = {
        "qt_use": [ns["st_A_end"]] * 4,
        "pt_use": [0, 0],
        "wo_done": {-1: 0, -2: 0},
    }
    slot_exp = [0, 0, 0, 0]
    slot_pv = [0, 0, 0, 0]
    kbase = [0]
    sv_ob = ns["sv_A_end"]
    sv_wo_prev = 0
    nqt = [0, 0, 0, 0]
    unit = [0]
    nab = [0, 0, 0, 0]
    ab_prev = [("sab2", 0), ("sab3", 0)]
    nat = [0, 0]
    nrs = [0, 0]

    def attention(Q, h, g):
        nonlocal sv_ob
        nkt = 4 * Q + 4
        st_obank = 0
        sv_o1 = 0
        for br in range(2):
            ksl = kslab[2 * h + br]
            qb = (2 * unit[0] + br) % 4
            nqt[qb] += 1
            p.sqd(lambda sc, h=h, br=br, Q=Q, qb=qb: sc.dma_start(
                out=qt_buf[qb][:, :],
                in_=ns["qts"][128 * (2 * h + br):128 * (2 * h + br) + 128,
                              512 * Q:512 * Q + 512]), f"sqt{qb}",
                waits=[("st", st["qt_use"][qb]), ("sv", ns["sv_A_end"])] + QW_TOT)
            w_qt = (f"sqt{qb}", 16 * nqt[qb])

            def emit_scores_kk(kk):
                slot = (kbase[0] + kk) % 4
                pb = PB[slot // 2]
                s0 = 512 * (slot % 2)
                w = [("ssc", slot_exp[slot]), w_qt]
                if kbase[0] + kk < 4:
                    w.append(("sv", ns["sv_A_end"]))
                if kk < 4 and br == 0:
                    w.append(("sv", sv_wo_prev))
                p.ten(lambda t, pb=pb, s0=s0, kk=kk:
                      t.matmul(pb[:, s0:s0 + 512],
                               ksl[:, 128 * kk:128 * kk + 128],
                               qt_buf[qb][:, :], start=True, stop=True),
                      inc=True, waits=w)
                st_sc = p.n["st"]
                p.sca(lambda sc, pb=pb, s0=s0, slot=slot: sc.activation(
                    pt_buf[slot // 2][:, 512 * (slot % 2):512 * (slot % 2) + 512],
                    pb[:, s0:s0 + 512], AF.Exp,
                    bias=c[:, 0:1], scale=SCALE),
                    waits=[("st", max(st_sc, slot_pv[slot])),
                           ("sv", ns["sv_negm"])])
                slot_exp[slot] = p.n["ssc"]
                exp_done[kk] = p.n["ssc"]
                r = kk - 4 * Q
                mask_done[kk] = 0
                if 0 <= r < 4:
                    p.vec(lambda v, slot=slot, r=r: v.tensor_mul(
                        pt_buf[slot // 2][:, 512 * (slot % 2):512 * (slot % 2) + 512],
                        pt_buf[slot // 2][:, 512 * (slot % 2):512 * (slot % 2) + 512],
                        ns["masks_sb"][:, 512 * r:512 * r + 512]),
                        waits=[("ssc", exp_done[kk]), ("smask", 16)])
                    mask_done[kk] = p.n["sv"]

            def emit_pv_kk(kk):
                slot = (kbase[0] + kk) % 4
                for qc in range(4):
                    opb = (PB[2], PB[3])[qc // 2]
                    o0 = 512 * (qc % 2)
                    w = []
                    if qc == 0:
                        w = [("ssc", exp_done[kk])]
                        if mask_done[kk]:
                            w.append(("sv", mask_done[kk]))
                        if kk == 0:
                            w.append(("sv", sv_ob if br == 0 else sv_o1))
                    p.ten(lambda t, opb=opb, o0=o0, slot=slot, qc=qc, kk=kk:
                          t.matmul(opb[:, o0:o0 + 257],
                                   pt_buf[slot // 2][:, 512 * (slot % 2) + 128 * qc:512 * (slot % 2) + 128 * qc + 128],
                                   vbuf[h][:, 257 * kk:257 * kk + 257],
                                   start=(kk == 0), stop=(kk == nkt - 1)),
                          inc=(qc == 3), waits=w)
                slot_pv[slot] = p.n["st"]

            exp_done = {}
            mask_done = {}
            LAG = 2
            for kk in range(nkt):
                emit_scores_kk(kk)
                if kk >= LAG:
                    emit_pv_kk(kk - LAG)
            for kk in range(nkt - LAG, nkt):
                emit_pv_kk(kk)
            kbase[0] += nkt
            st["qt_use"][qb] = p.n["st"]
            st_obank = p.n["st"]
            oev = (o1ev, o2ev)[br]
            for qc in range(4):
                opb = (PB[2], PB[3])[qc // 2]
                o0 = 512 * (qc % 2)
                p.vec(lambda v, qc=qc, opb=opb, o0=o0, oev=oev: v.tensor_copy(
                    oev[:, 257 * qc:257 * qc + 257], opb[:, o0:o0 + 257]),
                    waits=[("st", st_obank)])
            if br == 0:
                sv_o1 = p.n["sv"]
        u = unit[0]
        au = a_unit[u % 2]
        sv_ob_psum = [p.n["sv"]]
        for qc in range(4):
            p.vec(lambda v, qc=qc: v.reciprocal(
                c[:, 2:3], o1ev[:, 257 * qc + 256:257 * qc + 257]))
            p.vec(lambda v, qc=qc: v.tensor_scalar_mul(
                ajunk[:, :], o1ev[:, 257 * qc:257 * qc + 256], c[:, 2:3]))
            p.vec(lambda v, qc=qc: v.reciprocal(
                c[:, 3:4], o2ev[:, 257 * qc + 256:257 * qc + 257]))
            p.vec(lambda v, qc=qc: v.tensor_scalar_mul(
                acomb[:, :], o2ev[:, 257 * qc:257 * qc + 256], c[:, 3:4]))
            p.vec(lambda v: v.scalar_tensor_tensor(
                acomb[:, :], acomb[:, :], c[:, 1:2], ajunk[:, :],
                ALU.mult, ALU.add))
            p.vec(lambda v: v.scalar_tensor_tensor(
                ajunk[:, :], acomb[:, :], 1.0, acomb[:, :],
                ALU.mult, ALU.mult, accum_out=c[:, 4:5]))
            sv_s = p.n["sv"]
            p.sca(lambda sc: sc.activation(c[:, 5:6], c[:, 4:5], AF.Ln,
                                           scale=1.0 / 256.0, bias=c[:, 32:33]),
                  waits=[("sv", sv_s)])
            p.sca(lambda sc: sc.activation(c[:, 6:7], c[:, 5:6], AF.Exp, scale=-0.5))
            ssc_r = p.n["ssc"]
            p.vec(lambda v: v.tensor_scalar_mul(acomb[:, :], acomb[:, :], c[:, 6:7]),
                  waits=[("ssc", ssc_r)])
            p.vec(lambda v, au=au, qc=qc: v.tensor_mul(
                au[:, 256 * qc:256 * qc + 256], acomb[:, :], ns["subw_sb"][:, :]),
                waits=[ab_prev[u % 2]] if qc == 0 else [])
        sv_au = p.n["sv"]
        sem = f"sab{u % 4}"
        p.syn(lambda s, Q=Q, h=h, au=au: s.dma_start(
            out=ns["a_nat"][512 * Q:512 * Q + 512, 256 * h:256 * h + 256]
                .rearrange("(cc pp) w -> pp cc w", pp=128),
            in_=au[:, :].rearrange("p (cc w) -> p cc w", w=256)), sem,
            waits=[("sv", sv_au)])
        nab[u % 4] += 1
        ab_prev[u % 2] = (sem, 16 * nab[u % 4])
        unit[0] += 1
        sv_ob = sv_ob_psum[0]

    def at_reads(g):
        nat[g % 2] += 8
        for ca in range(4):
            for half in range(2):
                w = []
                if ca == 0 and half == 0:
                    w = [(f"sab{i}", 16 * (g + 1)) for i in range(4)]
                    w.append(("st", st["wo_done"][g - 2]))
                p.syn(lambda s, g=g, ca=ca, half=half: s.dma_start(
                    out=at_sb[g % 2][:, 1024 * ca + 512 * half:1024 * ca + 512 * half + 512],
                    in_=ns["a_nat"][1024 * g + 512 * half:1024 * g + 512 * half + 512,
                                    128 * ca:128 * ca + 128],
                    transpose=True), f"sat{g % 2}", waits=w)

    def wo_group(g):
        nonlocal sv_ob, sv_wo_prev
        sv_ev = {-2: 0, -1: 0}
        w_at = (f"sat{g % 2}", 16 * nat[g % 2])
        ssc_exp_last = max(slot_exp)
        for tl in range(8):
            tok0 = 1024 * g + 128 * tl
            pbs = (PB[2], PB[3]) if tl % 2 == 0 else (PB[0], PB[1])
            for ca in range(4):
                for ng in range(4):
                    pb = pbs[ng // 2]
                    s0 = 512 * (ng % 2)
                    w = []
                    if ca == 0 and ng == 0:
                        w = [w_at, ("swo", 16)]
                        if tl == 0:
                            w.append(("sv", sv_ob))
                        elif tl == 1:
                            w.append(("ssc", ssc_exp_last))
                        else:
                            w.append(("sv", sv_ev[tl - 2]))
                    p.ten(lambda t, pb=pb, s0=s0, ca=ca, ng=ng, tl=tl, g=g:
                          t.matmul(pb[:, s0:s0 + 512],
                                   at_sb[g % 2][:, 1024 * ca + 128 * tl:1024 * ca + 128 * tl + 128],
                                   wo_sb[:, E * ca + 512 * ng:E * ca + 512 * ng + 512],
                                   start=(ca == 0), stop=(ca == 3)),
                          inc=(ca == 3 and ng == 3), waits=w)
            st_wo = p.n["st"]
            ws = wo_stage[tl % 2]
            for ng in range(4):
                pb = pbs[ng // 2]
                s0 = 512 * (ng % 2)
                p.vec(lambda v, ws=ws, ng=ng, pb=pb, s0=s0: v.tensor_copy(
                    ws[:, 512 * ng:512 * ng + 512], pb[:, s0:s0 + 512]),
                    waits=[("st", st_wo),
                           (f"srs{tl % 2}", 16 * nrs[tl % 2])])
            sv_ev[tl] = p.n["sv"]
            nrs[tl % 2] += 1
            p.syn(lambda s, ws=ws, tok0=tok0: s.dma_start(
                out=ns["rs_in"][tok0:tok0 + 128, :], in_=ws[:, :]), f"srs{tl % 2}",
                waits=[("sv", sv_ev[tl])])
        sv_ob = p.n["sv"]
        sv_wo_prev = p.n["sv"]

    def issue_rs_chunk(k):
        w = [("srs0", 16 * nrs[0]), ("srs1", 16 * nrs[1])]
        if p.which == "gpsimd":
            p._w(w)
            if "fakecc" in ns["debug"]:
                p.eng.dma_start(out=ns["rs_out"][k],
                                in_=ns["rs_in"][2048 * k:2048 * k + TQ // 4, :]
                                ).then_inc(ns["sems"][f"cc{k}"], 16)
            else:
                p.eng.collective_compute(
                    "ReduceScatter", ALU.add,
                    replica_groups=[[0, 1, 2, 3], [4, 5, 6, 7]],
                    ins=[ns["rs_in"][2048 * k:2048 * k + 2048, :].opt()],
                    outs=[ns["rs_out"][k].opt()],
                ).then_inc(ns["sems"][f"cc{k}"])

    for g in range(8):
        for Q in (2 * g, 2 * g + 1):
            for h in range(2):
                attention(Q, h, g)
        at_reads(g)
        if g >= 1:
            wo_group(g - 1)
            st["wo_done"][g - 1] = p.n["st"]
            if (g - 1) % 2 == 1:
                issue_rs_chunk((g - 1) // 2)
    wo_group(7)
    st["wo_done"][7] = p.n["st"]
    issue_rs_chunk(3)
    ns["st_B_end"] = p.n["st"]
    ns["sv_B_end"] = p.n["sv"]
    ns["NAB"] = tuple(nab)


def emit_phase2a(p, ns):
    nc = ns["nc"]
    c = ns["smalls"]

    def alloc2a():
        ns["ab_scope"].close()
        ns.setdefault("closed", set()).add("ab_scope")
        s2a = contextlib.ExitStack()
        ns["s2a"] = s2a
        ns["xres_sb"] = [s2a.enter_context(nc.sbuf_tensor(f"xres{i}", [128, E], BF16)) for i in range(2)]
        ns["rs_sb"] = [s2a.enter_context(nc.sbuf_tensor("rssb0", [128, E], BF16))] * 2
        ns["x2pre"] = s2a.enter_context(nc.sbuf_tensor("x2pre", [128, E], F32))
        ns["junkE"] = s2a.enter_context(nc.sbuf_tensor("junkE", [128, E], F32))
        ns["x2bf"] = [s2a.enter_context(nc.sbuf_tensor("x2bf0", [128, E], BF16))] * 2
    alloc_once(ns, "2a", alloc2a)
    xres_sb, rs_sb = ns["xres_sb"], ns["rs_sb"]
    x2pre, junkE, x2bf = ns["x2pre"], ns["junkE"], ns["x2bf"]

    sv_use = [0, 0]
    sv_stt_prev = [0]
    n2a = [0, 0]
    nxr = [0, 0]
    nx2w = [0, 0]
    AB_TOT = [(f"sab{i}", 16 * ns["NAB"][i]) for i in range(4)]
    space_guard = ([("st", ns["st_B_end"]), ("sv", ns["sv_B_end"]),
                    ("sat0", p.n["sat0"]), ("sat1", p.n["sat1"]),
                    ("sqt0", p.n["sqt0"]), ("sqt1", p.n["sqt1"])] + AB_TOT)
    for t_ in range(16):
        bi = t_ % 2
        wnd = t_ // 4
        wcc = (f"cc{wnd}", ns["CCV"])
        w0 = list(space_guard) if t_ < 2 else []
        p.gps(lambda g, t_=t_, bi=bi: g.dma_start(
            out=xres_sb[bi][:, :], in_=ns["x_res"][128 * t_:128 * t_ + 128, :]),
            f"sxr{bi}", waits=[wcc, ("sv", sv_use[bi])] + w0)
        p.syn(lambda s, t_=t_, bi=bi, wnd=wnd: s.dma_start(
            out=rs_sb[bi][:, :],
            in_=ns["rs_out"][wnd][128 * (t_ % 4):128 * (t_ % 4) + 128, :]),
            f"s2a{bi}", waits=[wcc, ("sv", sv_stt_prev[0])] + w0)
        n2a[bi] += 1
        nxr[bi] += 1
        w_t = (f"s2a{bi}", 16 * n2a[bi])
        w_r = (f"sxr{bi}", 16 * nxr[bi])
        p.vec(lambda v, bi=bi: v.scalar_tensor_tensor(
            x2pre[:, :], rs_sb[bi][:, :], 1.0, xres_sb[bi][:, :],
            ALU.mult, ALU.add, accum_out=c[:, 8:9]),
            waits=[w_t, w_r])
        sv_stt_prev[0] = p.n["sv"]
        p.vec(lambda v: v.scalar_tensor_tensor(
            junkE[:, :], x2pre[:, :], 1.0, x2pre[:, :],
            ALU.mult, ALU.mult, accum_out=c[:, 9:10]))
        p.vec(lambda v: v.tensor_scalar_mul(c[:, 10:11], c[:, 8:9], -1.0 / E))
        p.vec(lambda v: v.tensor_scalar_mul(c[:, 11:12], c[:, 9:10], 1.0 / E))
        p.vec(lambda v: v.scalar_tensor_tensor(
            c[:, 12:13], c[:, 10:11], c[:, 10:11], c[:, 11:12],
            ALU.mult, ALU.subtract))
        sv_t = p.n["sv"]
        p.sca(lambda sc: sc.activation(c[:, 13:14], c[:, 12:13], AF.Ln,
                                       scale=-1.0, bias=c[:, 32:33]),
              waits=[("sv", sv_t)])
        p.sca(lambda sc: sc.activation(c[:, 14:15], c[:, 13:14], AF.Exp, scale=-0.5))
        ssc_t = p.n["ssc"]
        p.vec(lambda v, bi=bi: v.tensor_scalar(
            x2bf[bi][:, :], x2pre[:, :], c[:, 10:11], c[:, 14:15],
            ALU.add, ALU.mult),
            waits=[("ssc", ssc_t), ("sx2w0", 16 * nx2w[0]),
                   ("sx2w1", 16 * nx2w[1])])
        sv_use[bi] = p.n["sv"]
        nx2w[bi] += 1
        p.syn(lambda s, t_=t_, bi=bi: s.dma_start(
            out=ns["x2s"][128 * t_:128 * t_ + 128, :], in_=x2bf[bi][:, :]),
            f"sx2w{bi}", waits=[("sv", sv_use[bi])])
    ns["X2W_TOT"] = [("sx2w0", 16 * nx2w[0]), ("sx2w1", 16 * nx2w[1])]
    ns["X2W_BLK0"] = [("sx2w0", 16 * 4), ("sx2w1", 16 * 4)]
    ns["sv_2a_end"] = p.n["sv"]


def emit_phase2b(p, ns):
    nc = ns["nc"]
    PB = ns["PB"]
    c = ns["smalls"]

    def alloc2b():
        s2b = contextlib.ExitStack()
        ns["s2b"] = s2b
        ns["yblk"] = s2b.enter_context(nc.sbuf_tensor("yblk", [128, 8 * E], F32))
        ns["x2t"] = s2b.enter_context(nc.sbuf_tensor("x2t", [128, 16 * 1024], BF16))
        ns["hT"] = s2b.enter_context(nc.sbuf_tensor("hT", [128, 8 * 1024], BF16))
        ns["hsil"] = s2b.enter_context(nc.sbuf_tensor("hsil", [128, 1024], BF16))
        ns["wgs"] = [s2b.enter_context(nc.sbuf_tensor(f"wgs{i}", [128, E], BF16)) for i in range(2)]
        ns["wus"] = [s2b.enter_context(nc.sbuf_tensor(f"wus{i}", [128, E], BF16)) for i in range(2)]
        ns["wdf"] = s2b.enter_context(nc.sbuf_tensor("wdf", [128, 8 * E], BF16))
        ns["rw_sb"] = s2b.enter_context(nc.sbuf_tensor("rw_sb", [128, 16 * NE], BF16))
        ns["call"] = s2b.enter_context(nc.sbuf_tensor("call", [128, 80], F32))
        ns["x2rb"] = ns["xres_sb"][0]
        ns["x3pre"] = ns["x2pre"]
        ns["outst"] = [ns["junkE"]] * 2
    alloc_once(ns, "2b", alloc2b)
    yblk, x2t, hT, hsil = ns["yblk"], ns["x2t"], ns["hT"], ns["hsil"]
    wgs, wus, wdf = ns["wgs"], ns["wus"], ns["wdf"]
    rw_sb, call = ns["rw_sb"], ns["call"]
    x2rb, x3pre, outst = ns["x2rb"], ns["x3pre"], ns["outst"]
    junk32 = x2t[:, 0:4096].bitcast(F32)
    X2W = ns["X2W_TOT"]
    X2W0 = ns["X2W_BLK0"]

    p.gps(lambda g: g.dma_start(
        out=rw_sb[:, :].rearrange("p (dd e) -> p dd e", e=NE),
        in_=ns["router_w"].rearrange("(dd p) e -> p dd e", p=128)), "srw",
        waits=X2W0)

    passes = []
    for e in range(NE):
        passes.append((ns["w_gate"][e], ns["w_up"][e], ns["w_down"][e], e))
    passes.append((ns["sw_gate"][:, 0:1024], ns["sw_up"][:, 0:1024],
                   ns["sw_down"][0:1024, :], None))
    passes.append((ns["sw_gate"][:, 1024:2048], ns["sw_up"][:, 1024:2048],
                   ns["sw_down"][1024:2048, :], None))

    st_blk_done = 0
    sv_yuse_prev = ns["sv_2a_end"]
    sv_ln_prev = ns["sv_2a_end"]
    sv_x2rb_use = 0
    st_gu_use = {"g": [0, 0], "u": [0, 0]}
    st_wd_use = 0
    ssc_hsil = 0
    sv_hmul_last = 0
    nx2t = 0
    nwd = 0
    nwg = [0, 0]
    nwu = [0, 0]
    nx2rb = 0
    nout = [0, 0]

    for blk in range(2):
        tok0 = 1024 * blk
        for dd in range(16):
            w = []
            if dd == 0:
                w = (X2W0 if blk == 0 else X2W) + [
                    ("st", st_blk_done), ("sv", sv_ln_prev)]
            p.syn(lambda s, dd=dd, tok0=tok0: s.dma_start(
                out=x2t[:, 1024 * dd:1024 * dd + 1024],
                in_=ns["x2s"][tok0:tok0 + 1024, 128 * dd:128 * dd + 128],
                transpose=True), "sx2t", waits=w)
        nx2t += 16
        w_x2t = ("sx2t", 16 * nx2t)

        sv_rcopy = {-2: 0, -1: 0}
        for tl in range(8):
            r0 = 512 * (tl % 2)
            for dd in range(16):
                w = []
                if dd == 0:
                    w = [w_x2t, ("srw", 16)]
                    if tl <= 1:
                        w.append(("sv", sv_yuse_prev))
                    else:
                        w.append(("sv", sv_rcopy[tl - 2]))
                p.ten(lambda t, tl=tl, dd=dd, r0=r0: t.matmul(
                    PB[3][:, r0:r0 + NE],
                    x2t[:, 1024 * dd + 128 * tl:1024 * dd + 128 * tl + 128],
                    rw_sb[:, NE * dd:NE * dd + NE],
                    start=(dd == 0), stop=(dd == 15)),
                    inc=(dd == 15), waits=w)
            st_r = p.n["st"]
            p.vec(lambda v, r0=r0: v.tensor_copy(call[:, 64:72], PB[3][:, r0:r0 + NE]),
                  waits=[("st", st_r)])
            sv_rcopy[tl] = p.n["sv"] + 0
            p.vec(lambda v: v.reduce_max(c[:, 16:17], call[:, 64:72], AX.X))
            p.vec(lambda v: v.tensor_single_scalar(call[:, 72:80], call[:, 64:72],
                                                   c[:, 16:17], ALU.is_ge))
            p.vec(lambda v: v.scalar_tensor_tensor(call[:, 72:80], call[:, 72:80],
                                                   1e30, call[:, 64:72],
                                                   ALU.mult, ALU.subtract))
            p.vec(lambda v: v.tensor_reduce(c[:, 17:18], call[:, 72:80],
                                            AX.X, ALU.min))
            p.vec(lambda v: v.tensor_add(c[:, 18:19], c[:, 16:17], c[:, 17:18]))
            sv_r = p.n["sv"]
            p.sca(lambda sc: sc.activation(c[:, 19:20], c[:, 18:19], AF.Sigmoid),
                  waits=[("sv", sv_r)])
            ssc_r = p.n["ssc"]
            p.vec(lambda v: v.tensor_scalar_mul(c[:, 20:21], c[:, 17:18], -1.0),
                  waits=[("ssc", ssc_r)])
            p.vec(lambda v: v.tensor_single_scalar(call[:, 72:80], call[:, 64:72],
                                                   c[:, 16:17], ALU.is_ge))
            p.vec(lambda v: v.tensor_single_scalar(call[:, 64:72], call[:, 64:72],
                                                   c[:, 20:21], ALU.is_ge))
            p.vec(lambda v: v.tensor_scalar(c[:, 21:22], c[:, 19:20], 2.0, -1.0,
                                            ALU.mult, ALU.add))
            p.vec(lambda v: v.tensor_scalar(c[:, 22:23], c[:, 19:20], -1.0, 1.0,
                                            ALU.mult, ALU.add))
            p.vec(lambda v: v.tensor_scalar_mul(call[:, 72:80], call[:, 72:80],
                                                c[:, 21:22]))
            p.vec(lambda v, tl=tl: v.scalar_tensor_tensor(
                call[:, 8 * tl:8 * tl + 8], call[:, 64:72], c[:, 22:23],
                call[:, 72:80], ALU.mult, ALU.add))
        sv_router = p.n["sv"]

        sv_y_tl = [0] * 8
        sv_y_prev = sv_router
        for pe, (gsrc, usrc, dsrc, comb) in enumerate(passes):
            nwd += 1
            w_wd = ("swd", 16 * nwd)
            for hh in range(8):
                if hh == 7:
                    p.gps(lambda g, dsrc=dsrc: g.dma_start(
                        out=wdf[:, :].rearrange("p (hh cc) -> p hh cc", cc=E),
                        in_=dsrc.rearrange("(hh pp) cc -> pp hh cc", pp=128)), "swd",
                        waits=[("st", st_wd_use)] + (X2W0 if blk == 0 else X2W))
                hb = hh % 2
                nwg[hb] += 1
                p.gps(lambda g, gsrc=gsrc, hh=hh, hb=hb: g.dma_start(
                    out=wgs[hb][:, :].rearrange("p (dd cc) -> p dd cc", cc=128),
                    in_=gsrc[:, 128 * hh:128 * hh + 128].rearrange(
                        "(dd pp) cc -> pp dd cc", pp=128)), f"swg{hb}",
                    waits=[("st", st_gu_use["g"][hb])] + (X2W0 if blk == 0 else X2W))
                nwu[hb] += 1
                p.gps(lambda g, usrc=usrc, hh=hh, hb=hb: g.dma_start(
                    out=wus[hb][:, :].rearrange("p (dd cc) -> p dd cc", cc=128),
                    in_=usrc[:, 128 * hh:128 * hh + 128].rearrange(
                        "(dd pp) cc -> pp dd cc", pp=128)), f"swu{hb}",
                    waits=[("st", st_gu_use["u"][hb])] + (X2W0 if blk == 0 else X2W))
                for which, pb, wsb, wsem, wcnt in (
                        ("g", PB[0], wgs[hb], f"swg{hb}", nwg[hb]),
                        ("u", PB[1], wus[hb], f"swu{hb}", nwu[hb])):
                    for dd in range(16):
                        for half in range(2):
                            w = []
                            if dd == 0 and half == 0:
                                w = [(wsem, 16 * wcnt), w_x2t,
                                     ("sv", sv_y_prev)]
                                if which == "g":
                                    w.append(("ssc", ssc_hsil))
                                else:
                                    w.append(("sv", sv_hmul_last))
                            p.ten(lambda t, pb=pb, wsb=wsb, half=half, dd=dd:
                                  t.matmul(pb[:, 512 * half:512 * half + 512],
                                           wsb[:, 128 * dd:128 * dd + 128],
                                           x2t[:, 1024 * dd + 512 * half:1024 * dd + 512 * half + 512],
                                           start=(dd == 0), stop=(dd == 15)),
                                  inc=(dd == 15 and half == 1), waits=w)
                    st_gu_use[which][hb] = p.n["st"]
                st_gu = p.n["st"]
                p.sca(lambda sc: sc.activation(hsil[:, :], PB[0][:, :], AF.Silu),
                      waits=[("st", st_gu), ("sv", sv_hmul_last)])
                ssc_hsil = p.n["ssc"]
                p.vec(lambda v, hh=hh: v.tensor_tensor(
                    hT[:, 1024 * hh:1024 * hh + 1024], hsil[:, :], PB[1][:, :],
                    ALU.mult),
                    waits=[("ssc", ssc_hsil), ("st", st_wd_use)])
                sv_hmul_last = p.n["sv"]
            sv_hmul = sv_hmul_last
            sv_dn_ev = {-2: 0, -1: 0}
            for tl in range(8):
                pbs = (PB[2], PB[3]) if tl % 2 == 0 else (PB[0], PB[1])
                for hh in range(8):
                    for ng in range(4):
                        pb = pbs[ng // 2]
                        s0 = 512 * (ng % 2)
                        w = []
                        if hh == 0 and ng == 0:
                            w = [("sv", sv_hmul), w_wd]
                            if tl <= 1:
                                w.append(("sv", sv_y_prev))
                                if tl == 1:
                                    w.append(("ssc", ssc_hsil))
                            else:
                                w.append(("sv", sv_dn_ev[tl - 2]))
                            if pe == 0:
                                w.append(("sv", sv_router))
                        p.ten(lambda t, pb=pb, s0=s0, hh=hh, tl=tl, ng=ng:
                              t.matmul(pb[:, s0:s0 + 512],
                                       hT[:, 1024 * hh + 128 * tl:1024 * hh + 128 * tl + 128],
                                       wdf[:, E * hh + 512 * ng:E * hh + 512 * ng + 512],
                                       start=(hh == 0), stop=(hh == 7)),
                              inc=(hh == 7 and ng == 3), waits=w)
                st_dn = p.n["st"]
                for h2 in range(2):
                    pbsrc = pbs[h2]
                    ysl = slice(2048 * tl + 1024 * h2, 2048 * tl + 1024 * h2 + 1024)
                    if comb is None:
                        p.vec(lambda v, pbsrc=pbsrc, ysl=ysl: v.tensor_add(
                            yblk[:, ysl], yblk[:, ysl], pbsrc[:, :]),
                            waits=[("st", st_dn)])
                    else:
                        sc1 = call[:, 8 * tl + comb:8 * tl + comb + 1]
                        if pe == 0:
                            p.vec(lambda v, pbsrc=pbsrc, ysl=ysl, sc1=sc1:
                                  v.tensor_scalar_mul(yblk[:, ysl], pbsrc[:, :], sc1),
                                  waits=[("st", st_dn)])
                        else:
                            p.vec(lambda v, pbsrc=pbsrc, ysl=ysl, sc1=sc1:
                                  v.scalar_tensor_tensor(
                                      yblk[:, ysl], pbsrc[:, :], sc1, yblk[:, ysl],
                                      ALU.mult, ALU.add),
                                  waits=[("st", st_dn)])
                sv_y_tl[tl] = p.n["sv"]
                sv_dn_ev[tl] = p.n["sv"]
            sv_y_prev = p.n["sv"]
            st_wd_use = p.n["st"]
        st_blk_done = p.n["st"]

        for tl in range(8):
            t_glob = 8 * blk + tl
            bi = tl % 2
            nx2rb += 1
            p.syn(lambda s, t_glob=t_glob: s.dma_start(
                out=x2rb[:, :], in_=ns["x2s"][128 * t_glob:128 * t_glob + 128, :]),
                "sx2rb", waits=[("sv", sv_x2rb_use)] + X2W)
            w_t = ("sx2rb", 16 * nx2rb)
            p.vec(lambda v, tl=tl: v.scalar_tensor_tensor(
                x3pre[:, :], x2rb[:, :], 1.0, yblk[:, 2048 * tl:2048 * tl + 2048],
                ALU.mult, ALU.add, accum_out=c[:, 24:25]),
                waits=[w_t, ("sv", sv_y_tl[tl])])
            sv_x2rb_use = p.n["sv"]
            p.vec(lambda v: v.scalar_tensor_tensor(
                junk32[:, :], x3pre[:, :], 1.0, x3pre[:, :],
                ALU.mult, ALU.mult, accum_out=c[:, 25:26]))
            p.vec(lambda v: v.tensor_scalar_mul(c[:, 26:27], c[:, 24:25], -1.0 / E))
            p.vec(lambda v: v.tensor_scalar_mul(c[:, 27:28], c[:, 25:26], 1.0 / E))
            p.vec(lambda v: v.scalar_tensor_tensor(
                c[:, 28:29], c[:, 26:27], c[:, 26:27], c[:, 27:28],
                ALU.mult, ALU.subtract))
            sv_t = p.n["sv"]
            p.sca(lambda sc: sc.activation(c[:, 29:30], c[:, 28:29], AF.Ln,
                                           scale=-1.0, bias=c[:, 32:33]),
                  waits=[("sv", sv_t)])
            p.sca(lambda sc: sc.activation(c[:, 30:31], c[:, 29:30], AF.Exp,
                                           scale=-0.5))
            ssc_t = p.n["ssc"]
            p.vec(lambda v, bi=bi: v.tensor_scalar(
                outst[bi][:, :], x3pre[:, :], c[:, 26:27], c[:, 30:31],
                ALU.add, ALU.mult),
                waits=[("ssc", ssc_t), ("sout0", 16 * nout[0]),
                       ("sout1", 16 * nout[1])])
            sv_o = p.n["sv"]
            nout[bi] += 1
            p.syn(lambda s, t_glob=t_glob, bi=bi: s.dma_start(
                out=ns["out"][128 * t_glob:128 * t_glob + 128, :],
                in_=outst[bi][:, :]), f"sout{bi}", waits=[("sv", sv_o)])
        sv_yuse_prev = p.n["sv"]
        sv_ln_prev = p.n["sv"]


def emit_end(p, ns):
    gp = [s for s in DMA_SEMS if s[:3] in ("sxc",) or s in
          ("sconst", "swqk", "swv", "swo", "smask", "swd", "srw")
          or s[:3] in ("swg", "swu")]
    sy = [s for s in DMA_SEMS if s[:3] in ("sxt", "sat", "srs", "s2a") or s in
          ("sx2t", "sx2rb") or s[:4] in ("sx2w", "sout")]
    sq = [s for s in DMA_SEMS if s[:3] in ("sqw", "sqt", "sab")]
    p.wait("gpsimd", [(s, p.n[s]) for s in gp])
    p.wait("sync", [(s, p.n[s]) for s in sy])
    p.wait("scalar", [(s, p.n[s]) for s in sq])


def emit_dbg_phase1(p, ns):
    d = ns["dbg"]
    QW_TOT = [(f"sqw{i}", 16 * NT) for i in range(4)]
    for i in range(4):
        p.syn(lambda s, i=i: s.dma_start(
            out=d["d_kt"][128 * i:128 * i + 128, :], in_=ns["kslab"][i][:, :]),
            "sout0", waits=[("st", ns["st_A_end"]), ("sv", ns["sv_A_end"])])
    for h in range(2):
        p.syn(lambda s, h=h: s.dma_start(
            out=d["d_v"][128 * h:128 * h + 128, :], in_=ns["vbuf"][h][:, :]),
            "sout0", waits=[("sv", ns["sv_A_end"]), ("st", ns["st_A_end"])])
    p.syn(lambda s: s.dma_start(out=d["d_negm"], in_=ns["smalls"][:, 0:1]),
          "sout0", waits=[("sv", ns["sv_negm"])])
    p.syn(lambda s: s.dma_start(out=d["d_neglam"], in_=ns["smalls"][:, 1:2]),
          "sout0", waits=[("sv", ns["sv_preload"])])
    p.syn(lambda s: s.dma_start(out=d["d_qts"], in_=ns["qts"]),
          "sout0", waits=QW_TOT)


def emit_dbg_attn(p, ns):
    d = ns["dbg"]
    AB_TOT = [(f"sab{i}", 16 * ns["NAB"][i]) for i in range(4)]
    p.syn(lambda s: s.dma_start(out=d["d_anat"], in_=ns["a_nat"]),
          "sout1", waits=AB_TOT)
    p.syn(lambda s: s.dma_start(out=d["d_rsin"], in_=ns["rs_in"]),
          "sout1", waits=[("srs0", 16 * ns["NRS"][0]), ("srs1", 16 * ns["NRS"][1])])


def emit_dbg_x2(p, ns):
    d = ns["dbg"]
    p.syn(lambda s: s.dma_start(out=d["d_x2"], in_=ns["x2s"]),
          "sout0", waits=list(ns["X2W_TOT"]))



def _make_masks():
    m = np.zeros((128, 4 * 512), dtype=ml_dtypes.bfloat16)
    kl = np.arange(128)[:, None]
    ql = np.arange(512)[None, :]
    for r in range(4):
        m[:, 512 * r:512 * r + 512] = ((128 * r + kl) <= ql).astype(ml_dtypes.bfloat16)
    return m


_BUILD_CACHE = {}


def get_nc(debug=()):
    key = tuple(sorted(debug))
    if key not in _BUILD_CACHE:
        _BUILD_CACHE[key] = build(key)
    return _BUILD_CACHE[key]


def make_in_maps(inputs):
    x = np.ascontiguousarray(np.asarray(inputs["L0_attn_x"], dtype=np.float32))
    Wq = np.asarray(inputs["L0_attn_Wq"], dtype=np.float32)
    Wk = np.asarray(inputs["L0_attn_Wk"], dtype=np.float32)
    Wv = np.asarray(inputs["L0_attn_Wv"], dtype=np.float32)
    Wo = np.asarray(inputs["L0_attn_Wo"], dtype=np.float32)
    subw = np.ascontiguousarray(np.broadcast_to(
        np.asarray(inputs["L0_attn_subln_weight"], np.float32)[None, :], (128, 256)))
    masks = _make_masks()
    ones_col = np.ones((128, 1), dtype=ml_dtypes.bfloat16)
    ones_row = np.ones((1, 128), dtype=ml_dtypes.bfloat16)
    shared = {
        "subw": subw, "masks": masks, "ones_col": ones_col, "ones_row": ones_row,
        "lam_q1": np.asarray(inputs["L0_attn_lambda_q1"], np.float32).reshape(128, 1),
        "lam_k1": np.asarray(inputs["L0_attn_lambda_k1"], np.float32).reshape(128, 1),
        "lam_q2": np.asarray(inputs["L0_attn_lambda_q2"], np.float32).reshape(128, 1),
        "lam_k2": np.asarray(inputs["L0_attn_lambda_k2"], np.float32).reshape(128, 1),
        "router_w": np.asarray(inputs["L0_ffn_router_w"], np.float32),
        "w_gate": np.asarray(inputs["L0_ffn_w_gate"], np.float32),
        "w_up": np.asarray(inputs["L0_ffn_w_up"], np.float32),
        "w_down": np.asarray(inputs["L0_ffn_w_down"], np.float32),
        "sw_gate": np.asarray(inputs["L0_ffn_sw_gate"], np.float32),
        "sw_up": np.asarray(inputs["L0_ffn_sw_up"], np.float32),
        "sw_down": np.asarray(inputs["L0_ffn_sw_down"], np.float32),
    }
    in_maps = []
    for cix in range(8):
        b, j = cix // 4, cix % 4
        m = dict(shared)
        m["x_full"] = x[b]
        m["x_res"] = np.ascontiguousarray(np.concatenate(
            [x[b, 2048 * k + 512 * j:2048 * k + 512 * j + 512, :] for k in range(4)],
            axis=0))
        m["wq"] = np.ascontiguousarray(Wq[:, 512 * j:512 * j + 512])
        m["wk"] = np.ascontiguousarray(Wk[:, 512 * j:512 * j + 512])
        m["wv"] = np.ascontiguousarray(Wv[:, 512 * j:512 * j + 512])
        m["wo_rows"] = np.ascontiguousarray(Wo[512 * j:512 * j + 512, :])
        in_maps.append(m)
    return in_maps


def kernel(**inputs):
    nc = get_nc()
    in_maps = make_in_maps(inputs)
    res = run_bass_kernel_spmd(nc, in_maps, core_ids=list(range(8)))
    out = np.empty((B, S, E), dtype=np.float32)
    for cix in range(8):
        b, j = cix // 4, cix % 4
        o = res.results[cix]["out"]
        for k in range(4):
            out[b, 2048 * k + 512 * j:2048 * k + 512 * j + 512, :] = \
                o[512 * k:512 * k + 512]
    return out


# version 09
# speedup vs baseline: 1.1250x; 1.1250x over previous
import math
import contextlib
import numpy as np
import ml_dtypes
import concourse.bass as bass
import concourse.mybir as mybir
from concourse.bass_utils import run_bass_kernel_spmd

F32 = mybir.dt.float32
BF16 = mybir.dt.bfloat16
AF = mybir.ActivationFunctionType
ALU = mybir.AluOpType
AX = mybir.AxisListType

B, S, E = 2, 8192, 2048
SJ = 512
TQ = 2048
NE = 8
LAMBDA_INIT = 0.8 - 0.6 * float(np.exp(-0.3 * 12))
SCALE = 1.0 / math.sqrt(128.0)
EPS = 1e-5
NT = 16

DMA_SEMS = (
    "sxc0", "sxc1", "sxc2", "sxc3", "sconst", "swqk", "swv", "swo", "smask",
    "sqw0", "sqw1", "sqw2", "sqw3", "sxt0", "sxt1", "sqt0", "sqt1",
    "sab0", "sab1", "sab2", "sab3", "sat0", "sat1", "srs0", "srs1",
    "sqt2", "sqt3",
    "s2a0", "s2a1", "sx2w0", "sx2w1", "sx2t", "swg0", "swg1", "swu0", "swu1",
    "swd", "srw", "sout0", "sout1", "sx2rb",
    "sxc4", "sxc5", "sxc6", "sxc7", "sxr0", "sxr1",
)
SEMS = DMA_SEMS + ("st", "sv", "ssc",
                   "cc0", "cc1", "cc2", "cc3", "cc4", "cc5", "cc6", "cc7")


class Pass:
    def __init__(self, which, eng, sems):
        self.which = which
        self.eng = eng
        self.sems = sems
        self.n = {s: 0 for s in SEMS}

    def _w(self, waits):
        seen = {}
        for s, v in waits:
            if v > 0:
                seen[s] = max(seen.get(s, 0), v)
        for s, v in seen.items():
            self.eng.wait_ge(self.sems[s], v)

    def op(self, eng_name, fn, inc=None, waits=()):
        if eng_name == self.which:
            self._w(waits)
            ins = fn(self.eng)
            if inc is not None and ins is not None:
                ins.then_inc(self.sems[inc[0]], inc[1])
        if inc is not None:
            self.n[inc[0]] += inc[1]

    def ten(self, fn, inc=False, waits=()):
        self.op("tensor", fn, inc=("st", 1) if inc else None, waits=waits)

    def vec(self, fn, waits=()):
        self.op("vector", fn, inc=("sv", 1),
                waits=[("sv", self.n["sv"])] + list(waits))

    def sca(self, fn, waits=()):
        self.op("scalar", fn, inc=("ssc", 1),
                waits=[("ssc", self.n["ssc"])] + list(waits))

    def gps(self, fn, sem, waits=()):
        self.op("gpsimd", fn, inc=(sem, 16), waits=waits)

    def syn(self, fn, sem, waits=()):
        self.op("sync", fn, inc=(sem, 16), waits=waits)

    def sqd(self, fn, sem, waits=()):
        self.op("scalar", fn, inc=(sem, 16), waits=waits)

    def wait(self, eng_name, waits):
        if eng_name == self.which:
            self._w(waits)


def build(debug=()):
    nc = bass.Bass("TRN2", target_bir_lowering=False)

    def din(name, shape, dtype=F32):
        return nc.dram_tensor(name, shape, dtype, kind="ExternalInput").ap()

    def dout(name, shape, dtype=F32):
        return nc.dram_tensor(name, shape, dtype, kind="ExternalOutput").ap()

    def dint(name, shape, dtype=BF16):
        return nc.dram_tensor(name, shape, dtype, kind="Internal").ap()

    ns = {}
    ns["debug"] = set(debug)
    ns["CCV"] = 16 if "fakecc" in set(debug) else 1
    ns["x_full"] = din("x_full", [S, E])
    ns["x_res"] = din("x_res", [TQ, E])
    for nm in ("wq", "wk", "wv"):
        ns[nm] = din(nm, [E, SJ])
    ns["wo_rows"] = din("wo_rows", [SJ, E])
    for nm in ("lam_q1", "lam_k1", "lam_q2", "lam_k2"):
        ns[nm] = din(nm, [128, 1])
    ns["subw"] = din("subw", [128, 256])
    ns["masks"] = din("masks", [128, 4 * 512], BF16)
    ns["ones_col"] = din("ones_col", [128, 1], BF16)
    ns["ones_row"] = din("ones_row", [1, 128], BF16)
    ns["router_w"] = din("router_w", [E, NE])
    ns["w_gate"] = din("w_gate", [NE, E, 1024])
    ns["w_up"] = din("w_up", [NE, E, 1024])
    ns["w_down"] = din("w_down", [NE, 1024, E])
    ns["sw_gate"] = din("sw_gate", [E, 2048])
    ns["sw_up"] = din("sw_up", [E, 2048])
    ns["sw_down"] = din("sw_down", [2048, E])
    ns["out"] = dout("out", [TQ, E])

    ns["xbf"] = dint("xbf", [S, E])
    ns["qts"] = dint("qts", [4 * 16 * 128, 512])
    ns["a_nat"] = dint("a_nat", [S, SJ])
    ns["rs_in"] = dint("rs_in", [S, E])
    ns["rs_out"] = [dint(f"rs_out{k}", [TQ // 8, E]) for k in range(8)]
    ns["x2s"] = dint("x2s", [TQ, E])

    dbg = {}
    if "phase1" in ns["debug"]:
        dbg["d_kt"] = dout("d_kt", [4 * 128, S], BF16)
        dbg["d_v"] = dout("d_v", [2 * 128, 64 * 257], BF16)
        dbg["d_negm"] = dout("d_negm", [128, 1], F32)
        dbg["d_neglam"] = dout("d_neglam", [128, 1], F32)
        dbg["d_qts"] = dout("d_qts", [4 * 128, S], BF16)
    if "attn" in ns["debug"]:
        dbg["d_anat"] = dout("d_anat", [S, SJ], BF16)
        dbg["d_rsin"] = dout("d_rsin", [S, E], BF16)
    if "x2" in ns["debug"]:
        dbg["d_x2"] = dout("d_x2", [TQ, E], BF16)
    ns["dbg"] = dbg

    with contextlib.ExitStack() as ctx:
        E_ = ctx.enter_context
        sems = {s: E_(nc.semaphore(s)) for s in SEMS}
        ns["sems"] = sems
        block = E_(nc.Block())

        ns["smalls"] = E_(nc.sbuf_tensor("smalls", [128, 64], F32))
        ns["tiny"] = E_(nc.sbuf_tensor("tiny", [1, 16], F32))
        ns["tinyb"] = E_(nc.sbuf_tensor("tinyb", [1, 16], BF16))
        ns["onescol_sb"] = E_(nc.sbuf_tensor("onescol_sb", [128, 1], BF16))
        ns["onesrow_sb"] = E_(nc.sbuf_tensor("onesrow_sb", [1, 128], BF16))
        ns["subw_sb"] = E_(nc.sbuf_tensor("subw_sb", [128, 256], F32))
        for nm in ("lam_q1", "lam_k1", "lam_q2", "lam_k2"):
            ns[nm + "_sb"] = E_(nc.sbuf_tensor(nm + "_sb", [128, 1], BF16))
        ns["strip"] = E_(nc.sbuf_tensor("strip", [1, 1024], F32))
        ns["PB"] = [E_(nc.psum_tensor(f"PB{i}", [128, 1024], F32)) for i in range(4)]

        ab_scope = contextlib.ExitStack()
        ns["kslab"] = [ab_scope.enter_context(nc.sbuf_tensor(f"kslab{i}", [128, S], BF16)) for i in range(4)]
        ns["vbuf"] = [ab_scope.enter_context(nc.sbuf_tensor(f"vbuf{h}", [128, 64 * 257], BF16)) for h in range(2)]
        a_scope = contextlib.ExitStack()
        ns["wq_sb"] = a_scope.enter_context(nc.sbuf_tensor("wq_sb", [128, 16 * SJ], BF16))
        ns["wk_sb"] = a_scope.enter_context(nc.sbuf_tensor("wk_sb", [128, 16 * SJ], BF16))
        ns["xt"] = [a_scope.enter_context(nc.sbuf_tensor(f"xt{i}", [128, 16 * 512], BF16)) for i in range(2)]
        ns["qstage"] = [a_scope.enter_context(nc.sbuf_tensor(f"qstage{i}", [128, 512], BF16)) for i in range(2)]
        ns["qsq"] = [a_scope.enter_context(nc.sbuf_tensor(f"qsq{i}", [128, 512], BF16)) for i in range(2)]
        ns["a_scope"] = a_scope
        ns["ab_scope"] = ab_scope
        ns["nc"] = nc
        ns["alloc_done"] = set()

        counters = []

        def run_pass(which, eng):
            p = Pass(which, eng, sems)
            emit_all(p, ns)
            counters.append((which, dict(p.n)))

        @block.tensor
        def _(t):
            run_pass("tensor", t)

        @block.vector
        def _(v):
            run_pass("vector", v)

        @block.scalar
        def _(sc):
            run_pass("scalar", sc)

        @block.sync
        def _(s):
            run_pass("sync", s)

        @block.gpsimd
        def _(g):
            run_pass("gpsimd", g)

        ref = counters[0][1]
        for which, n in counters[1:]:
            assert n == ref, f"counter mismatch {which}: {n} vs {ref}"
        for key in ("s2b", "s2a", "a_scope", "ab_scope"):
            sc = ns.get(key)
            if sc is not None and key not in ns.get("closed", set()):
                try:
                    sc.close()
                except Exception:
                    pass
    return nc


def alloc_once(ns, key, fn):
    if key not in ns["alloc_done"]:
        ns["alloc_done"].add(key)
        fn()



def emit_all(p, ns):
    emit_preload(p, ns)
    emit_phaseA(p, ns)
    if "stopA" in ns["debug"]:
        emit_dbg_phase1(p, ns)
        emit_end(p, ns)
        return
    emit_phaseB(p, ns)
    if "phase1" in ns["debug"]:
        emit_dbg_phase1(p, ns)
    if "attn" in ns["debug"]:
        emit_dbg_attn(p, ns)
    if "stopB" in ns["debug"]:
        emit_end(p, ns)
        return
    emit_phase2a(p, ns)
    if "x2" in ns["debug"]:
        emit_dbg_x2(p, ns)
    if "stopx2" in ns["debug"]:
        emit_end(p, ns)
        return
    emit_phase2b(p, ns)
    emit_end(p, ns)


def emit_preload(p, ns):
    PB = ns["PB"]
    tiny, tinyb = ns["tiny"], ns["tinyb"]
    for part in range(8):
        p.gps(lambda g, part=part: g.dma_start(
            out=ns["xbf"][part * 1024:(part + 1) * 1024, :],
            in_=ns["x_full"][part * 1024:(part + 1) * 1024, :]), f"sxc{part}")
    p.gps(lambda g: g.dma_start(out=ns["onescol_sb"][:, :], in_=ns["ones_col"]), "sconst")
    p.gps(lambda g: g.dma_start(out=ns["onesrow_sb"][:, :], in_=ns["ones_row"]), "sconst")
    p.gps(lambda g: g.dma_start(out=ns["subw_sb"][:, :], in_=ns["subw"]), "sconst")
    for nm in ("lam_q1", "lam_k1", "lam_q2", "lam_k2"):
        p.gps(lambda g, nm=nm: g.dma_start(out=ns[nm + "_sb"][:, :], in_=ns[nm]), "sconst")
    NCONST = p.n["sconst"]
    ns["NCONST"] = NCONST

    for w_ap, sb in ((ns["wq"], ns["wq_sb"]), (ns["wk"], ns["wk_sb"])):
        p.gps(lambda g, w_ap=w_ap, sb=sb: g.dma_start(
            out=sb[:, :].rearrange("p (dd c) -> p dd c", c=512),
            in_=w_ap.rearrange("(dd p) c -> p dd c", p=128)), "swqk")

    p.vec(lambda v: v.memset(ns["smalls"][:, 32:33], EPS))
    p.vec(lambda v: v.tensor_scalar_mul(ns["subw_sb"][:, :], ns["subw_sb"][:, :],
                                        1.0 - LAMBDA_INIT),
          waits=[("sconst", NCONST)])

    p.ten(lambda t: t.matmul(PB[0][0:1, 0:1], ns["lam_q1_sb"][:, :],
                             ns["lam_k1_sb"][:, :], start=True, stop=True),
          waits=[("sconst", NCONST)])
    p.ten(lambda t: t.matmul(PB[1][0:1, 0:1], ns["lam_q2_sb"][:, :],
                             ns["lam_k2_sb"][:, :], start=True, stop=True), inc=True)
    st_dots = p.n["st"]
    p.sca(lambda sc: sc.activation(tiny[0:1, 3:4], PB[0][0:1, 0:1], AF.Exp),
          waits=[("st", st_dots)])
    p.sca(lambda sc: sc.activation(tiny[0:1, 4:5], PB[1][0:1, 0:1], AF.Exp))
    ssc_l = p.n["ssc"]
    p.vec(lambda v: v.tensor_sub(tiny[0:1, 2:3], tiny[0:1, 3:4], tiny[0:1, 4:5]),
          waits=[("ssc", ssc_l)])
    p.vec(lambda v: v.tensor_scalar(tiny[0:1, 2:3], tiny[0:1, 2:3], LAMBDA_INIT, -1.0,
                                    ALU.add, ALU.mult))
    p.vec(lambda v: v.tensor_copy(tinyb[0:1, 2:3], tiny[0:1, 2:3]))
    sv_l = p.n["sv"]
    p.ten(lambda t: t.matmul(PB[0][:, 0:1], ns["onesrow_sb"][0:1, :],
                             tinyb[0:1, 2:3], start=True, stop=True),
          inc=True, waits=[("sv", sv_l)])
    st_l = p.n["st"]
    p.vec(lambda v: v.tensor_copy(ns["smalls"][:, 1:2], PB[0][:, 0:1]),
          waits=[("st", st_l)])
    ns["sv_preload"] = p.n["sv"]


def emit_phaseA(p, ns):
    PB = ns["PB"]
    xt, qstage, qsq = ns["xt"], ns["qstage"], ns["qsq"]
    kslab, vbuf, strip = ns["kslab"], ns["vbuf"], ns["strip"]
    tiny, tinyb = ns["tiny"], ns["tinyb"]

    p.vec(lambda v: v.memset(strip[0:1, :], 0.0))

    st_tile_end = {-2: 0, -1: 0}
    sv_tile_end = {-1: ns["sv_preload"] + 1}
    st_strip = {-2: 0, -1: 0}
    strip_guard = [0, 0]
    nxt = [0, 0]

    for tt in range(NT):
        part = (tt * 512) // 1024
        for dd in range(16):
            w = []
            if dd == 0:
                w = [(f"sxc{part}", 16), ("st", st_tile_end[tt - 2])]
            p.syn(lambda s, tt=tt, dd=dd: s.dma_start(
                out=xt[tt % 2][:, 512 * dd:512 * dd + 512],
                in_=ns["xbf"][tt * 512:(tt + 1) * 512, 128 * dd:128 * dd + 128],
                transpose=True), f"sxt{tt % 2}", waits=w)
        nxt[tt % 2] += 16
        w_xt = (f"sxt{tt % 2}", 16 * nxt[tt % 2])
        xtile = xt[tt % 2]

        for grp in range(2):
            w_sb = (ns["wq_sb"], ns["wk_sb"])[grp]
            pbs = (PB[0], PB[1]) if grp == 0 else (PB[2], PB[3])
            for qc in range(4):
                pb = pbs[qc // 2]
                s0 = 512 * (qc % 2)
                for dd in range(16):
                    w = []
                    if dd == 0 and qc == 0:
                        w = [w_xt, ("swqk", 32),
                             ("sv", sv_tile_end[tt - 1])]
                    p.ten(lambda t, pb=pb, s0=s0, w_sb=w_sb, qc=qc, dd=dd, xtile=xtile:
                          t.matmul(pb[:, s0:s0 + 512],
                                   w_sb[:, 512 * dd + 128 * qc:512 * dd + 128 * qc + 128],
                                   xtile[:, 512 * dd:512 * dd + 512],
                                   start=(dd == 0), stop=(dd == 15)),
                          inc=(dd == 15), waits=w)
            st_grp = p.n["st"]
            for qc in range(4):
                pb = pbs[qc // 2]
                s0 = 512 * (qc % 2)
                if grp == 0:
                    if qc < 2:
                        wprev = (f"sqw{qc + 2}", 16 * tt)
                    else:
                        wprev = (f"sqw{qc - 2}", 16 * (tt + 1))
                    p.vec(lambda v, qc=qc, pb=pb, s0=s0: v.tensor_copy(
                        qstage[qc % 2][:, :], pb[:, s0:s0 + 512]),
                        waits=[("st", st_grp), wprev])
                    sv_ev = p.n["sv"]
                    p.sqd(lambda sc, qc=qc, tt=tt: sc.dma_start(
                        out=ns["qts"][128 * (16 * qc + tt):128 * (16 * qc + tt) + 128, :],
                        in_=qstage[qc % 2][:, :]), f"sqw{qc}",
                        waits=[("sv", sv_ev)])
                else:
                    p.vec(lambda v, qc=qc, pb=pb, s0=s0, tt=tt: v.tensor_copy(
                        kslab[qc][:, 512 * tt:512 * tt + 512], pb[:, s0:s0 + 512]),
                        waits=[("st", st_grp)])
            for qc in range(4):
                if grp == 0:
                    p.vec(lambda v, qc=qc: v.tensor_mul(
                        qsq[qc % 2][:, :], qstage[qc % 2][:, :], qstage[qc % 2][:, :]),
                        waits=[("st", strip_guard[qc % 2])])
                else:
                    p.vec(lambda v, qc=qc, tt=tt: v.tensor_mul(
                        qsq[qc % 2][:, :],
                        kslab[qc][:, 512 * tt:512 * tt + 512],
                        kslab[qc][:, 512 * tt:512 * tt + 512]),
                        waits=[("st", strip_guard[qc % 2])])
                sv_sq = p.n["sv"]
                s0 = 0 if grp == 0 else 512
                p.ten(lambda t, qc=qc, s0=s0: t.matmul(
                    PB[0][0:1, s0:s0 + 512], ns["onescol_sb"][:, :],
                    qsq[qc % 2][:, :], start=(qc == 0), stop=(qc == 3)),
                    inc=True, waits=[("sv", sv_sq)])
                strip_guard[qc % 2] = p.n["st"]
            if grp == 1:
                st_strip[tt] = p.n["st"]
        p.vec(lambda v: v.tensor_max(strip[0:1, 0:512], strip[0:1, 0:512],
                                     PB[0][0:1, 0:512]),
              waits=[("st", st_strip[tt])])
        p.vec(lambda v: v.tensor_max(strip[0:1, 512:1024], strip[0:1, 512:1024],
                                     PB[0][0:1, 512:1024]))
        st_tile_end[tt] = p.n["st"]
        sv_tile_end[tt] = p.n["sv"]

    p.vec(lambda v: v.reduce_max(tiny[0:1, 5:6], strip[0:1, 0:512], AX.X))
    p.vec(lambda v: v.reduce_max(tiny[0:1, 6:7], strip[0:1, 512:1024], AX.X))
    p.vec(lambda v: v.tensor_mul(tiny[0:1, 7:8], tiny[0:1, 5:6], tiny[0:1, 6:7]))
    sv_m = p.n["sv"]
    p.sca(lambda sc: sc.activation(tiny[0:1, 8:9], tiny[0:1, 7:8], AF.Ln),
          waits=[("sv", sv_m)])
    p.sca(lambda sc: sc.activation(tiny[0:1, 9:10], tiny[0:1, 8:9], AF.Exp, scale=0.5))
    ssc_m = p.n["ssc"]
    p.vec(lambda v: v.tensor_scalar_mul(tiny[0:1, 10:11], tiny[0:1, 9:10], -SCALE),
          waits=[("ssc", ssc_m)])
    p.vec(lambda v: v.tensor_copy(tinyb[0:1, 10:11], tiny[0:1, 10:11]))
    sv_m2 = p.n["sv"]
    p.ten(lambda t: t.matmul(PB[1][:, 0:1], ns["onesrow_sb"][0:1, :],
                             tinyb[0:1, 10:11], start=True, stop=True),
          inc=True, waits=[("sv", sv_m2)])
    st_m = p.n["st"]
    p.vec(lambda v: v.tensor_copy(ns["smalls"][:, 0:1], PB[1][:, 0:1]),
          waits=[("st", st_m)])
    ns["sv_negm"] = p.n["sv"]

    wv_sb = ns["wq_sb"]
    p.gps(lambda g: g.dma_start(
        out=wv_sb[:, :].rearrange("p (dd c) -> p dd c", c=512),
        in_=ns["wv"].rearrange("(dd p) c -> p dd c", p=128)), "swv",
        waits=[("st", st_tile_end[NT - 1])])

    for h in range(2):
        p.vec(lambda v, h=h: v.memset(
            vbuf[h][:, :].rearrange("p (t c) -> p t c", c=257)[:, :, 256], 1.0))
    sv_ones = p.n["sv"]

    st_vtile = {-2: st_tile_end[NT - 1], -1: st_tile_end[NT - 1]}
    sv_vtile = {-2: sv_ones, -1: sv_ones}
    for tt in range(NT):
        for dd in range(16):
            w = []
            if dd == 0:
                w = [("st", st_vtile[tt - 2])]
            p.syn(lambda s, tt=tt, dd=dd: s.dma_start(
                out=xt[tt % 2][:, 512 * dd:512 * dd + 512],
                in_=ns["xbf"][tt * 512:(tt + 1) * 512, 128 * dd:128 * dd + 128],
                transpose=True), f"sxt{tt % 2}", waits=w)
        nxt[tt % 2] += 16
        w_xt = (f"sxt{tt % 2}", 16 * nxt[tt % 2])
        xtile = xt[tt % 2]
        vpbs = (PB[0], PB[1]) if tt % 2 == 0 else (PB[2], PB[3])
        for i in range(4):
            pb = vpbs[i // 2]
            s0 = 512 * (i % 2)
            for dd in range(16):
                w = []
                if i == 0 and dd == 0:
                    w = [w_xt, ("swv", 16), ("sv", sv_vtile[tt - 2])]
                p.ten(lambda t, pb=pb, s0=s0, i=i, dd=dd, xtile=xtile:
                      t.matmul(pb[:, s0:s0 + 512],
                               xtile[:, 512 * dd + 128 * i:512 * dd + 128 * i + 128],
                               wv_sb[:, 512 * dd:512 * dd + 512],
                               start=(dd == 0), stop=(dd == 15)),
                      inc=(dd == 15), waits=w)
        st_grp = p.n["st"]
        for i in range(4):
            pb = vpbs[i // 2]
            s0 = 512 * (i % 2)
            kk = 4 * tt + i
            for h in range(2):
                p.vec(lambda v, pb=pb, s0=s0, h=h, kk=kk: v.tensor_copy(
                    vbuf[h][:, 257 * kk:257 * kk + 256],
                    pb[:, s0 + 256 * h:s0 + 256 * h + 256]),
                    waits=[("st", st_grp)])
        st_vtile[tt] = st_grp
        sv_vtile[tt] = p.n["sv"]
    ns["st_A_end"] = p.n["st"]
    ns["sv_A_end"] = p.n["sv"]


def emit_phaseB(p, ns):
    nc = ns["nc"]
    PB = ns["PB"]
    kslab, vbuf = ns["kslab"], ns["vbuf"]
    c = ns["smalls"]

    def allocB():
        ns["a_scope"].close()
        ns.setdefault("closed", set()).add("a_scope")
        sc_ = ns["ab_scope"].enter_context
        ns["qt_buf"] = [sc_(nc.sbuf_tensor(f"qt{i}", [128, 512], BF16)) for i in range(4)]
        ns["pt_buf"] = [sc_(nc.sbuf_tensor(f"pt{i}", [128, 1024], BF16)) for i in range(2)]
        ns["o1ev"] = sc_(nc.sbuf_tensor("o1ev", [128, 1028], F32))
        ns["o2ev"] = sc_(nc.sbuf_tensor("o2ev", [128, 1028], F32))
        ns["acomb4"] = sc_(nc.sbuf_tensor("acomb4", [128, 1024], F32))
        ns["ajunk"] = sc_(nc.sbuf_tensor("ajunk", [128, 256], F32))
        ns["a_unit"] = [sc_(nc.sbuf_tensor(f"aun{i}", [128, 1024], BF16)) for i in range(2)]
        ns["at_sb"] = [sc_(nc.sbuf_tensor(f"at{i}", [128, 4096], BF16)) for i in range(2)]
        ns["wo_sb"] = sc_(nc.sbuf_tensor("wo_sb", [128, 4 * E], BF16))
        ns["wo_stage"] = [sc_(nc.sbuf_tensor(f"wost{i}", [128, E], BF16)) for i in range(2)]
        ns["masks_sb"] = sc_(nc.sbuf_tensor("masks_sb", [128, 4 * 512], BF16))
    alloc_once(ns, "B", allocB)
    qt_buf, pt_buf = ns["qt_buf"], ns["pt_buf"]
    o1ev, acomb4, ajunk = ns["o1ev"], ns["acomb4"], ns["ajunk"]
    o2ev = ns["o2ev"]
    a_unit, at_sb = ns["a_unit"], ns["at_sb"]
    wo_sb, wo_stage = ns["wo_sb"], ns["wo_stage"]

    QW_TOT = [(f"sqw{i}", 16 * NT) for i in range(4)]
    w_space = [("st", ns["st_A_end"]), ("sv", ns["sv_A_end"])] + QW_TOT
    p.gps(lambda g: g.dma_start(
        out=wo_sb[:, :].rearrange("p (ca cc) -> p ca cc", cc=E),
        in_=ns["wo_rows"].rearrange("(ca pp) cc -> pp ca cc", pp=128)), "swo",
        waits=w_space)
    p.gps(lambda g: g.dma_start(out=ns["masks_sb"][:, :], in_=ns["masks"]),
          "smask", waits=w_space)

    st = {
        "qt_use": [ns["st_A_end"]] * 4,
        "pt_use": [0, 0],
        "wo_done": {-1: 0, -2: 0},
    }
    slot_exp = [0, 0, 0, 0]
    slot_pv = [0, 0, 0, 0]
    kbase = [0]
    sv_ob = ns["sv_A_end"]
    sv_wo_prev = 0
    nqt = [0, 0, 0, 0]
    qt_cnt = {}
    st_kslab0_free = [0]
    unit = [0]
    nab = [0, 0, 0, 0]
    ab_prev = [("sab2", 0), ("sab3", 0)]
    nat = [0, 0]
    nrs = [0, 0]

    def attention(Q, h, g):
        nonlocal sv_ob
        nkt = 4 * Q + 4
        st_obank = 0
        sv_o1 = 0
        qt_load(unit[0] + 1)
        for br in range(2):
            ksl = kslab[2 * h + br]
            qb = (2 * unit[0] + br) % 4
            w_qt = (f"sqt{qb}", 16 * qt_cnt[(unit[0], br)])

            def emit_scores_kk(kk):
                slot = (kbase[0] + kk) % 4
                pb = PB[slot // 2]
                s0 = 512 * (slot % 2)
                w = [("ssc", slot_exp[slot]), w_qt]
                if kbase[0] + kk < 4:
                    w.append(("sv", ns["sv_A_end"]))
                if kk < 4 and br == 0:
                    w.append(("sv", sv_wo_prev))
                p.ten(lambda t, pb=pb, s0=s0, kk=kk:
                      t.matmul(pb[:, s0:s0 + 512],
                               ksl[:, 128 * kk:128 * kk + 128],
                               qt_buf[qb][:, :], start=True, stop=True),
                      inc=True, waits=w)
                st_sc = p.n["st"]
                p.sca(lambda sc, pb=pb, s0=s0, slot=slot: sc.activation(
                    pt_buf[slot // 2][:, 512 * (slot % 2):512 * (slot % 2) + 512],
                    pb[:, s0:s0 + 512], AF.Exp,
                    bias=c[:, 0:1], scale=SCALE),
                    waits=[("st", max(st_sc, slot_pv[slot])),
                           ("sv", ns["sv_negm"])])
                slot_exp[slot] = p.n["ssc"]
                exp_done[kk] = p.n["ssc"]
                r = kk - 4 * Q
                mask_done[kk] = 0
                if 0 <= r < 4:
                    p.vec(lambda v, slot=slot, r=r: v.tensor_mul(
                        pt_buf[slot // 2][:, 512 * (slot % 2):512 * (slot % 2) + 512],
                        pt_buf[slot // 2][:, 512 * (slot % 2):512 * (slot % 2) + 512],
                        ns["masks_sb"][:, 512 * r:512 * r + 512]),
                        waits=[("ssc", exp_done[kk]), ("smask", 16)])
                    mask_done[kk] = p.n["sv"]

            def emit_pv_kk(kk):
                slot = (kbase[0] + kk) % 4
                for qc in range(4):
                    opb = (PB[2], PB[3])[qc // 2]
                    o0 = 512 * (qc % 2)
                    w = []
                    if qc == 0:
                        w = [("ssc", exp_done[kk])]
                        if mask_done[kk]:
                            w.append(("sv", mask_done[kk]))
                        if kk == 0:
                            w.append(("sv", sv_ob if br == 0 else sv_o1))
                    p.ten(lambda t, opb=opb, o0=o0, slot=slot, qc=qc, kk=kk:
                          t.matmul(opb[:, o0:o0 + 257],
                                   pt_buf[slot // 2][:, 512 * (slot % 2) + 128 * qc:512 * (slot % 2) + 128 * qc + 128],
                                   vbuf[h][:, 257 * kk:257 * kk + 257],
                                   start=(kk == 0), stop=(kk == nkt - 1)),
                          inc=(qc == 3), waits=w)
                slot_pv[slot] = p.n["st"]

            exp_done = {}
            mask_done = {}
            LAG = 2
            for kk in range(nkt):
                emit_scores_kk(kk)
                if kk >= LAG:
                    emit_pv_kk(kk - LAG)
            for kk in range(nkt - LAG, nkt):
                emit_pv_kk(kk)
            kbase[0] += nkt
            st["qt_use"][qb] = p.n["st"]
            if h == 0 and br == 0:
                st_kslab0_free[0] = p.n["st"]
            st_obank = p.n["st"]
            oev = (o1ev, o2ev)[br]
            for qc in range(4):
                opb = (PB[2], PB[3])[qc // 2]
                o0 = 512 * (qc % 2)
                p.vec(lambda v, qc=qc, opb=opb, o0=o0, oev=oev: v.tensor_copy(
                    oev[:, 257 * qc:257 * qc + 257], opb[:, o0:o0 + 257]),
                    waits=[("st", st_obank)])
            if br == 0:
                sv_o1 = p.n["sv"]
        u = unit[0]
        au = a_unit[u % 2]
        sv_ob_psum = [p.n["sv"]]
        p.vec(lambda v: v.reciprocal(
            c[:, 36:40], o1ev[:, :].rearrange("p (q w) -> p q w", w=257)[:, :, 256]))
        p.vec(lambda v: v.reciprocal(
            c[:, 40:44], o2ev[:, :].rearrange("p (q w) -> p q w", w=257)[:, :, 256]))
        for qc in range(4):
            p.vec(lambda v, qc=qc: v.tensor_scalar_mul(
                ajunk[:, :], o1ev[:, 257 * qc:257 * qc + 256], c[:, 36 + qc:37 + qc]))
            p.vec(lambda v, qc=qc: v.tensor_scalar_mul(
                acomb4[:, 256 * qc:256 * qc + 256],
                o2ev[:, 257 * qc:257 * qc + 256], c[:, 40 + qc:41 + qc]))
            p.vec(lambda v, qc=qc: v.scalar_tensor_tensor(
                acomb4[:, 256 * qc:256 * qc + 256],
                acomb4[:, 256 * qc:256 * qc + 256], c[:, 1:2], ajunk[:, :],
                ALU.mult, ALU.add))
            p.vec(lambda v, qc=qc: v.scalar_tensor_tensor(
                ajunk[:, :], acomb4[:, 256 * qc:256 * qc + 256], 1.0,
                acomb4[:, 256 * qc:256 * qc + 256],
                ALU.mult, ALU.mult, accum_out=c[:, 44 + qc:45 + qc]))
        sv_s = p.n["sv"]
        p.sca(lambda sc: sc.activation(c[:, 48:52], c[:, 44:48], AF.Ln,
                                       scale=1.0 / 256.0, bias=c[:, 32:33]),
              waits=[("sv", sv_s)])
        p.sca(lambda sc: sc.activation(c[:, 52:56], c[:, 48:52], AF.Exp, scale=-0.5))
        ssc_r = p.n["ssc"]
        for qc in range(4):
            p.vec(lambda v, qc=qc: v.tensor_scalar_mul(
                acomb4[:, 256 * qc:256 * qc + 256],
                acomb4[:, 256 * qc:256 * qc + 256], c[:, 52 + qc:53 + qc]),
                waits=[("ssc", ssc_r)] if qc == 0 else [])
            p.vec(lambda v, au=au, qc=qc: v.tensor_mul(
                au[:, 256 * qc:256 * qc + 256],
                acomb4[:, 256 * qc:256 * qc + 256], ns["subw_sb"][:, :]),
                waits=[ab_prev[u % 2]] if qc == 0 else [])
        sv_au = p.n["sv"]
        sem = f"sab{u % 4}"
        p.syn(lambda s, Q=Q, h=h, au=au: s.dma_start(
            out=ns["a_nat"][512 * Q:512 * Q + 512, 256 * h:256 * h + 256]
                .rearrange("(cc pp) w -> pp cc w", pp=128),
            in_=au[:, :].rearrange("p (cc w) -> p cc w", w=256)), sem,
            waits=[("sv", sv_au)])
        nab[u % 4] += 1
        ab_prev[u % 2] = (sem, 16 * nab[u % 4])
        unit[0] += 1
        sv_ob = sv_ob_psum[0]

    def at_reads(g):
        nat[g % 2] += 8
        for ca in range(4):
            for half in range(2):
                w = []
                if ca == 0 and half == 0:
                    w = [(f"sab{i}", 16 * (g + 1)) for i in range(4)]
                    w.append(("st", st["wo_done"][g - 2]))
                p.syn(lambda s, g=g, ca=ca, half=half: s.dma_start(
                    out=at_sb[g % 2][:, 1024 * ca + 512 * half:1024 * ca + 512 * half + 512],
                    in_=ns["a_nat"][1024 * g + 512 * half:1024 * g + 512 * half + 512,
                                    128 * ca:128 * ca + 128],
                    transpose=True), f"sat{g % 2}", waits=w)

    def wo_group(g):
        nonlocal sv_ob, sv_wo_prev
        sv_ev = {-2: 0, -1: 0}
        w_at = (f"sat{g % 2}", 16 * nat[g % 2])
        ssc_exp_last = max(slot_exp)
        for tl in range(8):
            tok0 = 1024 * g + 128 * tl
            pbs = (PB[2], PB[3]) if tl % 2 == 0 else (PB[0], PB[1])
            for ca in range(4):
                for ng in range(4):
                    pb = pbs[ng // 2]
                    s0 = 512 * (ng % 2)
                    w = []
                    if ca == 0 and ng == 0:
                        w = [w_at, ("swo", 16)]
                        if tl == 0:
                            w.append(("sv", sv_ob))
                        elif tl == 1:
                            w.append(("ssc", ssc_exp_last))
                        else:
                            w.append(("sv", sv_ev[tl - 2]))
                    p.ten(lambda t, pb=pb, s0=s0, ca=ca, ng=ng, tl=tl, g=g:
                          t.matmul(pb[:, s0:s0 + 512],
                                   at_sb[g % 2][:, 1024 * ca + 128 * tl:1024 * ca + 128 * tl + 128],
                                   wo_sb[:, E * ca + 512 * ng:E * ca + 512 * ng + 512],
                                   start=(ca == 0), stop=(ca == 3)),
                          inc=(ca == 3 and ng == 3), waits=w)
            st_wo = p.n["st"]
            ws = wo_stage[tl % 2]
            for ng in range(4):
                pb = pbs[ng // 2]
                s0 = 512 * (ng % 2)
                p.vec(lambda v, ws=ws, ng=ng, pb=pb, s0=s0: v.tensor_copy(
                    ws[:, 512 * ng:512 * ng + 512], pb[:, s0:s0 + 512]),
                    waits=[("st", st_wo),
                           (f"srs{tl % 2}", 16 * nrs[tl % 2])])
            sv_ev[tl] = p.n["sv"]
            nrs[tl % 2] += 1
            p.syn(lambda s, ws=ws, tok0=tok0: s.dma_start(
                out=ns["rs_in"][tok0:tok0 + 128, :], in_=ws[:, :]), f"srs{tl % 2}",
                waits=[("sv", sv_ev[tl])])
        sv_ob = p.n["sv"]
        sv_wo_prev = p.n["sv"]

    def issue_rs_chunk(k):
        w = [("srs0", 16 * nrs[0]), ("srs1", 16 * nrs[1])]
        if p.which == "gpsimd":
            p._w(w)
            if "fakecc" in ns["debug"]:
                p.eng.dma_start(out=ns["rs_out"][k],
                                in_=ns["rs_in"][1024 * k:1024 * k + TQ // 8, :]
                                ).then_inc(ns["sems"][f"cc{k}"], 16)
            else:
                p.eng.collective_compute(
                    "ReduceScatter", ALU.add,
                    replica_groups=[[0, 1, 2, 3], [4, 5, 6, 7]],
                    ins=[ns["rs_in"][1024 * k:1024 * k + 1024, :].opt()],
                    outs=[ns["rs_out"][k].opt()],
                ).then_inc(ns["sems"][f"cc{k}"])

    units = [(2 * g + dq, h, g) for g in range(8) for dq in (0, 1) for h in (0, 1)]

    def qt_load(uidx):
        if uidx >= len(units):
            return
        Q_, h_, _ = units[uidx]
        for br in range(2):
            qb = (2 * uidx + br) % 4
            nqt[qb] += 1
            qt_cnt[(uidx, br)] = nqt[qb]
            p.sqd(lambda sc, h_=h_, br=br, Q_=Q_, qb=qb: sc.dma_start(
                out=qt_buf[qb][:, :],
                in_=ns["qts"][128 * (16 * (2 * h_ + br) + Q_):
                              128 * (16 * (2 * h_ + br) + Q_) + 128, :]),
                f"sqt{qb}",
                waits=[("st", st["qt_use"][qb]), ("sv", ns["sv_A_end"])] + QW_TOT)

    qt_load(0)
    for g in range(8):
        for Q in (2 * g, 2 * g + 1):
            for h in range(2):
                attention(Q, h, g)
        at_reads(g)
        if g >= 1:
            wo_group(g - 1)
            st["wo_done"][g - 1] = p.n["st"]
            issue_rs_chunk(g - 1)
    wo_group(7)
    st["wo_done"][7] = p.n["st"]
    issue_rs_chunk(7)
    ns["st_B_end"] = p.n["st"]
    ns["sv_B_end"] = p.n["sv"]
    ns["st_kslab0_free"] = st_kslab0_free[0]
    ns["NAB"] = tuple(nab)


def emit_phase2a(p, ns):
    nc = ns["nc"]
    c = ns["smalls"]

    def alloc2a():
        ns["ab_scope"].close()
        ns.setdefault("closed", set()).add("ab_scope")
        s2a = contextlib.ExitStack()
        ns["s2a"] = s2a
        ns["xres_sb"] = [s2a.enter_context(nc.sbuf_tensor(f"xres{i}", [128, E], BF16)) for i in range(2)]
        ns["rs_sb"] = [s2a.enter_context(nc.sbuf_tensor("rssb0", [128, E], BF16))] * 2
        ns["x2pre"] = s2a.enter_context(nc.sbuf_tensor("x2pre", [128, E], F32))
        ns["junkE"] = s2a.enter_context(nc.sbuf_tensor("junkE", [128, E], F32))
        ns["x2bf"] = [s2a.enter_context(nc.sbuf_tensor("x2bf0", [128, E], BF16))] * 2
    alloc_once(ns, "2a", alloc2a)
    xres_sb, rs_sb = ns["xres_sb"], ns["rs_sb"]
    x2pre, junkE, x2bf = ns["x2pre"], ns["junkE"], ns["x2bf"]

    sv_use = [0, 0]
    sv_stt_prev = [0]
    n2a = [0, 0]
    nxr = [0, 0]
    nx2w = [0, 0]
    AB_TOT = [(f"sab{i}", 16 * ns["NAB"][i]) for i in range(4)]
    space_guard = [("st", ns["st_kslab0_free"])]
    for t_ in range(16):
        bi = t_ % 2
        wnd = t_ // 2
        wcc = (f"cc{wnd}", ns["CCV"])
        w0 = list(space_guard) if t_ < 2 else []
        p.gps(lambda g, t_=t_, bi=bi: g.dma_start(
            out=xres_sb[bi][:, :], in_=ns["x_res"][128 * t_:128 * t_ + 128, :]),
            f"sxr{bi}", waits=[wcc, ("sv", sv_use[bi])] + w0)
        p.syn(lambda s, t_=t_, bi=bi, wnd=wnd: s.dma_start(
            out=rs_sb[bi][:, :],
            in_=ns["rs_out"][wnd][128 * (t_ % 2):128 * (t_ % 2) + 128, :]),
            f"s2a{bi}", waits=[wcc, ("sv", sv_stt_prev[0])] + w0)
        n2a[bi] += 1
        nxr[bi] += 1
        w_t = (f"s2a{bi}", 16 * n2a[bi])
        w_r = (f"sxr{bi}", 16 * nxr[bi])
        p.vec(lambda v, bi=bi: v.scalar_tensor_tensor(
            x2pre[:, :], rs_sb[bi][:, :], 1.0, xres_sb[bi][:, :],
            ALU.mult, ALU.add, accum_out=c[:, 8:9]),
            waits=[w_t, w_r])
        sv_stt_prev[0] = p.n["sv"]
        sv_pre = p.n["sv"]
        p.sca(lambda sc: sc.activation(junkE[:, :], x2pre[:, :], AF.Square,
                                       accum_out=c[:, 9:10]),
              waits=[("sv", sv_pre)])
        ssc_sq = p.n["ssc"]
        p.vec(lambda v: v.tensor_scalar_mul(c[:, 10:11], c[:, 8:9], -1.0 / E))
        p.vec(lambda v: v.tensor_scalar_mul(c[:, 11:12], c[:, 9:10], 1.0 / E),
              waits=[("ssc", ssc_sq)])
        p.vec(lambda v: v.scalar_tensor_tensor(
            c[:, 12:13], c[:, 10:11], c[:, 10:11], c[:, 11:12],
            ALU.mult, ALU.subtract))
        sv_t = p.n["sv"]
        p.sca(lambda sc: sc.activation(c[:, 13:14], c[:, 12:13], AF.Ln,
                                       scale=-1.0, bias=c[:, 32:33]),
              waits=[("sv", sv_t)])
        p.sca(lambda sc: sc.activation(c[:, 14:15], c[:, 13:14], AF.Exp, scale=-0.5))
        ssc_t = p.n["ssc"]
        p.vec(lambda v, bi=bi: v.tensor_scalar(
            x2bf[bi][:, :], x2pre[:, :], c[:, 10:11], c[:, 14:15],
            ALU.add, ALU.mult),
            waits=[("ssc", ssc_t), ("sx2w0", 16 * nx2w[0]),
                   ("sx2w1", 16 * nx2w[1])])
        sv_use[bi] = p.n["sv"]
        nx2w[bi] += 1
        p.syn(lambda s, t_=t_, bi=bi: s.dma_start(
            out=ns["x2s"][128 * t_:128 * t_ + 128, :], in_=x2bf[bi][:, :]),
            f"sx2w{bi}", waits=[("sv", sv_use[bi])])
    ns["X2W_TOT"] = [("sx2w0", 16 * nx2w[0]), ("sx2w1", 16 * nx2w[1])]
    ns["X2W_BLK0"] = [("sx2w0", 16 * 4), ("sx2w1", 16 * 4)]
    ns["sv_2a_end"] = p.n["sv"]


def emit_phase2b(p, ns):
    nc = ns["nc"]
    PB = ns["PB"]
    c = ns["smalls"]

    def alloc2b():
        s2b = contextlib.ExitStack()
        ns["s2b"] = s2b
        ns["yblk"] = s2b.enter_context(nc.sbuf_tensor("yblk", [128, 8 * E], F32))
        ns["x2t"] = s2b.enter_context(nc.sbuf_tensor("x2t", [128, 16 * 1024], BF16))
        ns["hT"] = s2b.enter_context(nc.sbuf_tensor("hT", [128, 8 * 1024], BF16))
        ns["hsil"] = s2b.enter_context(nc.sbuf_tensor("hsil", [128, 1024], BF16))
        ns["wgs"] = [s2b.enter_context(nc.sbuf_tensor(f"wgs{i}", [128, E], BF16)) for i in range(2)]
        ns["wus"] = [s2b.enter_context(nc.sbuf_tensor(f"wus{i}", [128, E], BF16)) for i in range(2)]
        ns["wdf"] = s2b.enter_context(nc.sbuf_tensor("wdf", [128, 8 * E], BF16))
        ns["rw_sb"] = s2b.enter_context(nc.sbuf_tensor("rw_sb", [128, 16 * NE], BF16))
        ns["call"] = s2b.enter_context(nc.sbuf_tensor("call", [128, 80], F32))
        ns["x2rb"] = ns["xres_sb"][0]
        ns["x3pre"] = ns["x2pre"]
        ns["outst"] = [ns["junkE"]] * 2
    alloc_once(ns, "2b", alloc2b)
    yblk, x2t, hT, hsil = ns["yblk"], ns["x2t"], ns["hT"], ns["hsil"]
    wgs, wus, wdf = ns["wgs"], ns["wus"], ns["wdf"]
    rw_sb, call = ns["rw_sb"], ns["call"]
    x2rb, x3pre, outst = ns["x2rb"], ns["x3pre"], ns["outst"]
    junk32 = hT[:, 0:4096].bitcast(F32)
    X2W = ns["X2W_TOT"]
    X2W0 = ns["X2W_BLK0"]

    p.gps(lambda g: g.dma_start(
        out=rw_sb[:, :].rearrange("p (dd e) -> p dd e", e=NE),
        in_=ns["router_w"].rearrange("(dd p) e -> p dd e", p=128)), "srw",
        waits=X2W0)

    passes = []
    for e in range(NE):
        passes.append((ns["w_gate"][e], ns["w_up"][e], ns["w_down"][e], e))
    passes.append((ns["sw_gate"][:, 0:1024], ns["sw_up"][:, 0:1024],
                   ns["sw_down"][0:1024, :], None))
    passes.append((ns["sw_gate"][:, 1024:2048], ns["sw_up"][:, 1024:2048],
                   ns["sw_down"][1024:2048, :], None))

    st_blk_done = 0
    sv_dn_prev = ns["sv_2a_end"]
    sv_x2rb_use = 0
    st_gu_use = {"g": [0, 0], "u": [0, 0]}
    st_wd_use = 0
    ssc_hsil = 0
    sv_hmul_last = 0
    nx2t = 0
    nwd = 0
    nwg = [0, 0]
    nwu = [0, 0]
    nx2rb = 0
    nout = [0, 0]

    def x2t_reads(blk, st_guard):
        tok0 = 1024 * blk
        for dd in range(16):
            w = []
            if dd == 0:
                w = (X2W0 if blk == 0 else X2W) + [("st", st_guard)]
            p.syn(lambda s, dd=dd, tok0=tok0: s.dma_start(
                out=x2t[:, 1024 * dd:1024 * dd + 1024],
                in_=ns["x2s"][tok0:tok0 + 1024, 128 * dd:128 * dd + 128],
                transpose=True), "sx2t", waits=w)

    x2t_reads(0, 0)
    nx2t += 16
    for blk in range(2):
        w_x2t = ("sx2t", 16 * nx2t)

        sv_rcopy = {-2: 0, -1: 0}
        for tl in range(8):
            r0 = 512 * (tl % 2)
            for dd in range(16):
                w = []
                if dd == 0:
                    w = [w_x2t, ("srw", 16)]
                    if tl <= 1:
                        w.append(("sv", sv_dn_prev))
                    else:
                        w.append(("sv", sv_rcopy[tl - 2]))
                p.ten(lambda t, tl=tl, dd=dd, r0=r0: t.matmul(
                    PB[3][:, r0:r0 + NE],
                    x2t[:, 1024 * dd + 128 * tl:1024 * dd + 128 * tl + 128],
                    rw_sb[:, NE * dd:NE * dd + NE],
                    start=(dd == 0), stop=(dd == 15)),
                    inc=(dd == 15), waits=w)
            st_r = p.n["st"]
            p.vec(lambda v, r0=r0: v.tensor_copy(call[:, 64:72], PB[3][:, r0:r0 + NE]),
                  waits=[("st", st_r)])
            sv_rcopy[tl] = p.n["sv"] + 0
            p.vec(lambda v: v.reduce_max(c[:, 16:17], call[:, 64:72], AX.X))
            p.vec(lambda v: v.tensor_single_scalar(call[:, 72:80], call[:, 64:72],
                                                   c[:, 16:17], ALU.is_ge))
            p.vec(lambda v: v.scalar_tensor_tensor(call[:, 72:80], call[:, 72:80],
                                                   1e30, call[:, 64:72],
                                                   ALU.mult, ALU.subtract))
            p.vec(lambda v: v.tensor_reduce(c[:, 17:18], call[:, 72:80],
                                            AX.X, ALU.min))
            p.vec(lambda v: v.tensor_add(c[:, 18:19], c[:, 16:17], c[:, 17:18]))
            sv_r = p.n["sv"]
            p.sca(lambda sc: sc.activation(c[:, 19:20], c[:, 18:19], AF.Sigmoid),
                  waits=[("sv", sv_r)])
            ssc_r = p.n["ssc"]
            p.vec(lambda v: v.tensor_scalar_mul(c[:, 20:21], c[:, 17:18], -1.0),
                  waits=[("ssc", ssc_r)])
            p.vec(lambda v: v.tensor_single_scalar(call[:, 72:80], call[:, 64:72],
                                                   c[:, 16:17], ALU.is_ge))
            p.vec(lambda v: v.tensor_single_scalar(call[:, 64:72], call[:, 64:72],
                                                   c[:, 20:21], ALU.is_ge))
            p.vec(lambda v: v.tensor_scalar(c[:, 21:22], c[:, 19:20], 2.0, -1.0,
                                            ALU.mult, ALU.add))
            p.vec(lambda v: v.tensor_scalar(c[:, 22:23], c[:, 19:20], -1.0, 1.0,
                                            ALU.mult, ALU.add))
            p.vec(lambda v: v.tensor_scalar_mul(call[:, 72:80], call[:, 72:80],
                                                c[:, 21:22]))
            p.vec(lambda v, tl=tl: v.scalar_tensor_tensor(
                call[:, 8 * tl:8 * tl + 8], call[:, 64:72], c[:, 22:23],
                call[:, 72:80], ALU.mult, ALU.add))
        sv_router = p.n["sv"]

        sv_y_tl = [0] * 8
        sv_y_prev = sv_router
        for pe, (gsrc, usrc, dsrc, comb) in enumerate(passes):
            nwd += 1
            w_wd = ("swd", 16 * nwd)
            for hh in range(8):
                if hh == 7:
                    p.gps(lambda g, dsrc=dsrc: g.dma_start(
                        out=wdf[:, :].rearrange("p (hh cc) -> p hh cc", cc=E),
                        in_=dsrc.rearrange("(hh pp) cc -> pp hh cc", pp=128)), "swd",
                        waits=[("st", st_wd_use)] + (X2W0 if blk == 0 else X2W))
                hb = hh % 2
                nwg[hb] += 1
                p.gps(lambda g, gsrc=gsrc, hh=hh, hb=hb: g.dma_start(
                    out=wgs[hb][:, :].rearrange("p (dd cc) -> p dd cc", cc=128),
                    in_=gsrc[:, 128 * hh:128 * hh + 128].rearrange(
                        "(dd pp) cc -> pp dd cc", pp=128)), f"swg{hb}",
                    waits=[("st", st_gu_use["g"][hb])] + (X2W0 if blk == 0 else X2W))
                nwu[hb] += 1
                p.gps(lambda g, usrc=usrc, hh=hh, hb=hb: g.dma_start(
                    out=wus[hb][:, :].rearrange("p (dd cc) -> p dd cc", cc=128),
                    in_=usrc[:, 128 * hh:128 * hh + 128].rearrange(
                        "(dd pp) cc -> pp dd cc", pp=128)), f"swu{hb}",
                    waits=[("st", st_gu_use["u"][hb])] + (X2W0 if blk == 0 else X2W))
                for which, pb, wsb, wsem, wcnt in (
                        ("g", PB[0], wgs[hb], f"swg{hb}", nwg[hb]),
                        ("u", PB[1], wus[hb], f"swu{hb}", nwu[hb])):
                    for dd in range(16):
                        for half in range(2):
                            w = []
                            if dd == 0 and half == 0:
                                w = [(wsem, 16 * wcnt), w_x2t,
                                     ("sv", sv_y_prev)]
                                if which == "g":
                                    w.append(("ssc", ssc_hsil))
                                else:
                                    w.append(("sv", sv_hmul_last))
                            p.ten(lambda t, pb=pb, wsb=wsb, half=half, dd=dd:
                                  t.matmul(pb[:, 512 * half:512 * half + 512],
                                           wsb[:, 128 * dd:128 * dd + 128],
                                           x2t[:, 1024 * dd + 512 * half:1024 * dd + 512 * half + 512],
                                           start=(dd == 0), stop=(dd == 15)),
                                  inc=(dd == 15 and half == 1), waits=w)
                    st_gu_use[which][hb] = p.n["st"]
                st_gu = p.n["st"]
                p.sca(lambda sc: sc.activation(hsil[:, :], PB[0][:, :], AF.Silu),
                      waits=[("st", st_gu), ("sv", sv_hmul_last)])
                ssc_hsil = p.n["ssc"]
                p.vec(lambda v, hh=hh: v.tensor_tensor(
                    hT[:, 1024 * hh:1024 * hh + 1024], hsil[:, :], PB[1][:, :],
                    ALU.mult),
                    waits=[("ssc", ssc_hsil), ("st", st_wd_use)])
                sv_hmul_last = p.n["sv"]
            sv_hmul = sv_hmul_last
            sv_dn_ev = {-2: 0, -1: 0}
            for tl in range(8):
                pbs = (PB[2], PB[3]) if tl % 2 == 0 else (PB[0], PB[1])
                for hh in range(8):
                    for ng in range(4):
                        pb = pbs[ng // 2]
                        s0 = 512 * (ng % 2)
                        w = []
                        if hh == 0 and ng == 0:
                            w = [("sv", sv_hmul), w_wd]
                            if tl <= 1:
                                w.append(("sv", sv_y_prev))
                                if tl == 1:
                                    w.append(("ssc", ssc_hsil))
                            else:
                                w.append(("sv", sv_dn_ev[tl - 2]))
                            if pe == 0:
                                w.append(("sv", sv_router))
                        p.ten(lambda t, pb=pb, s0=s0, hh=hh, tl=tl, ng=ng:
                              t.matmul(pb[:, s0:s0 + 512],
                                       hT[:, 1024 * hh + 128 * tl:1024 * hh + 128 * tl + 128],
                                       wdf[:, E * hh + 512 * ng:E * hh + 512 * ng + 512],
                                       start=(hh == 0), stop=(hh == 7)),
                              inc=(hh == 7 and ng == 3), waits=w)
                st_dn = p.n["st"]
                for h2 in range(2):
                    pbsrc = pbs[h2]
                    ysl = slice(2048 * tl + 1024 * h2, 2048 * tl + 1024 * h2 + 1024)
                    if comb is None:
                        p.vec(lambda v, pbsrc=pbsrc, ysl=ysl: v.tensor_add(
                            yblk[:, ysl], yblk[:, ysl], pbsrc[:, :]),
                            waits=[("st", st_dn)])
                    else:
                        sc1 = call[:, 8 * tl + comb:8 * tl + comb + 1]
                        if pe == 0:
                            p.vec(lambda v, pbsrc=pbsrc, ysl=ysl, sc1=sc1:
                                  v.tensor_scalar_mul(yblk[:, ysl], pbsrc[:, :], sc1),
                                  waits=[("st", st_dn)])
                        else:
                            p.vec(lambda v, pbsrc=pbsrc, ysl=ysl, sc1=sc1:
                                  v.scalar_tensor_tensor(
                                      yblk[:, ysl], pbsrc[:, :], sc1, yblk[:, ysl],
                                      ALU.mult, ALU.add),
                                  waits=[("st", st_dn)])
                sv_y_tl[tl] = p.n["sv"]
                sv_dn_ev[tl] = p.n["sv"]
            sv_y_prev = p.n["sv"]
            st_wd_use = p.n["st"]
        st_blk_done = p.n["st"]
        sv_dn_prev = p.n["sv"]
        if blk == 0:
            x2t_reads(1, st_blk_done)
            nx2t += 16

        for tl in range(8):
            t_glob = 8 * blk + tl
            bi = tl % 2
            nx2rb += 1
            p.syn(lambda s, t_glob=t_glob: s.dma_start(
                out=x2rb[:, :], in_=ns["x2s"][128 * t_glob:128 * t_glob + 128, :]),
                "sx2rb", waits=[("sv", sv_x2rb_use)] + X2W)
            w_t = ("sx2rb", 16 * nx2rb)
            p.vec(lambda v, tl=tl: v.scalar_tensor_tensor(
                x3pre[:, :], x2rb[:, :], 1.0, yblk[:, 2048 * tl:2048 * tl + 2048],
                ALU.mult, ALU.add, accum_out=c[:, 24:25]),
                waits=[w_t, ("sv", sv_y_tl[tl])])
            sv_x2rb_use = p.n["sv"]
            p.vec(lambda v: v.scalar_tensor_tensor(
                junk32[:, :], x3pre[:, :], 1.0, x3pre[:, :],
                ALU.mult, ALU.mult, accum_out=c[:, 25:26]))
            p.vec(lambda v: v.tensor_scalar_mul(c[:, 26:27], c[:, 24:25], -1.0 / E))
            p.vec(lambda v: v.tensor_scalar_mul(c[:, 27:28], c[:, 25:26], 1.0 / E))
            p.vec(lambda v: v.scalar_tensor_tensor(
                c[:, 28:29], c[:, 26:27], c[:, 26:27], c[:, 27:28],
                ALU.mult, ALU.subtract))
            sv_t = p.n["sv"]
            p.sca(lambda sc: sc.activation(c[:, 29:30], c[:, 28:29], AF.Ln,
                                           scale=-1.0, bias=c[:, 32:33]),
                  waits=[("sv", sv_t)])
            p.sca(lambda sc: sc.activation(c[:, 30:31], c[:, 29:30], AF.Exp,
                                           scale=-0.5))
            ssc_t = p.n["ssc"]
            p.vec(lambda v, bi=bi: v.tensor_scalar(
                outst[bi][:, :], x3pre[:, :], c[:, 26:27], c[:, 30:31],
                ALU.add, ALU.mult),
                waits=[("ssc", ssc_t), ("sout0", 16 * nout[0]),
                       ("sout1", 16 * nout[1])])
            sv_o = p.n["sv"]
            nout[bi] += 1
            p.syn(lambda s, t_glob=t_glob, bi=bi: s.dma_start(
                out=ns["out"][128 * t_glob:128 * t_glob + 128, :],
                in_=outst[bi][:, :]), f"sout{bi}", waits=[("sv", sv_o)])



def emit_end(p, ns):
    gp = [s for s in DMA_SEMS if s[:3] in ("sxc",) or s in
          ("sconst", "swqk", "swv", "swo", "smask", "swd", "srw")
          or s[:3] in ("swg", "swu")]
    sy = [s for s in DMA_SEMS if s[:3] in ("sxt", "sat", "srs", "s2a") or s in
          ("sx2t", "sx2rb") or s[:4] in ("sx2w", "sout")]
    sq = [s for s in DMA_SEMS if s[:3] in ("sqw", "sqt", "sab")]
    p.wait("gpsimd", [(s, p.n[s]) for s in gp])
    p.wait("sync", [(s, p.n[s]) for s in sy])
    p.wait("scalar", [(s, p.n[s]) for s in sq])


def emit_dbg_phase1(p, ns):
    d = ns["dbg"]
    QW_TOT = [(f"sqw{i}", 16 * NT) for i in range(4)]
    for i in range(4):
        p.syn(lambda s, i=i: s.dma_start(
            out=d["d_kt"][128 * i:128 * i + 128, :], in_=ns["kslab"][i][:, :]),
            "sout0", waits=[("st", ns["st_A_end"]), ("sv", ns["sv_A_end"])])
    for h in range(2):
        p.syn(lambda s, h=h: s.dma_start(
            out=d["d_v"][128 * h:128 * h + 128, :], in_=ns["vbuf"][h][:, :]),
            "sout0", waits=[("sv", ns["sv_A_end"]), ("st", ns["st_A_end"])])
    p.syn(lambda s: s.dma_start(out=d["d_negm"], in_=ns["smalls"][:, 0:1]),
          "sout0", waits=[("sv", ns["sv_negm"])])
    p.syn(lambda s: s.dma_start(out=d["d_neglam"], in_=ns["smalls"][:, 1:2]),
          "sout0", waits=[("sv", ns["sv_preload"])])
    p.syn(lambda s: s.dma_start(out=d["d_qts"], in_=ns["qts"]),
          "sout0", waits=QW_TOT)


def emit_dbg_attn(p, ns):
    d = ns["dbg"]
    AB_TOT = [(f"sab{i}", 16 * ns["NAB"][i]) for i in range(4)]
    p.syn(lambda s: s.dma_start(out=d["d_anat"], in_=ns["a_nat"]),
          "sout1", waits=AB_TOT)
    p.syn(lambda s: s.dma_start(out=d["d_rsin"], in_=ns["rs_in"]),
          "sout1", waits=[("srs0", 16 * ns["NRS"][0]), ("srs1", 16 * ns["NRS"][1])])


def emit_dbg_x2(p, ns):
    d = ns["dbg"]
    p.syn(lambda s: s.dma_start(out=d["d_x2"], in_=ns["x2s"]),
          "sout0", waits=list(ns["X2W_TOT"]))



def _make_masks():
    m = np.zeros((128, 4 * 512), dtype=ml_dtypes.bfloat16)
    kl = np.arange(128)[:, None]
    ql = np.arange(512)[None, :]
    for r in range(4):
        m[:, 512 * r:512 * r + 512] = ((128 * r + kl) <= ql).astype(ml_dtypes.bfloat16)
    return m


_BUILD_CACHE = {}


def get_nc(debug=()):
    key = tuple(sorted(debug))
    if key not in _BUILD_CACHE:
        _BUILD_CACHE[key] = build(key)
    return _BUILD_CACHE[key]


def make_in_maps(inputs):
    x = np.ascontiguousarray(np.asarray(inputs["L0_attn_x"], dtype=np.float32))
    Wq = np.asarray(inputs["L0_attn_Wq"], dtype=np.float32)
    Wk = np.asarray(inputs["L0_attn_Wk"], dtype=np.float32)
    Wv = np.asarray(inputs["L0_attn_Wv"], dtype=np.float32)
    Wo = np.asarray(inputs["L0_attn_Wo"], dtype=np.float32)
    subw = np.ascontiguousarray(np.broadcast_to(
        np.asarray(inputs["L0_attn_subln_weight"], np.float32)[None, :], (128, 256)))
    masks = _make_masks()
    ones_col = np.ones((128, 1), dtype=ml_dtypes.bfloat16)
    ones_row = np.ones((1, 128), dtype=ml_dtypes.bfloat16)
    shared = {
        "subw": subw, "masks": masks, "ones_col": ones_col, "ones_row": ones_row,
        "lam_q1": np.asarray(inputs["L0_attn_lambda_q1"], np.float32).reshape(128, 1),
        "lam_k1": np.asarray(inputs["L0_attn_lambda_k1"], np.float32).reshape(128, 1),
        "lam_q2": np.asarray(inputs["L0_attn_lambda_q2"], np.float32).reshape(128, 1),
        "lam_k2": np.asarray(inputs["L0_attn_lambda_k2"], np.float32).reshape(128, 1),
        "router_w": np.asarray(inputs["L0_ffn_router_w"], np.float32),
        "w_gate": np.asarray(inputs["L0_ffn_w_gate"], np.float32),
        "w_up": np.asarray(inputs["L0_ffn_w_up"], np.float32),
        "w_down": np.asarray(inputs["L0_ffn_w_down"], np.float32),
        "sw_gate": np.asarray(inputs["L0_ffn_sw_gate"], np.float32),
        "sw_up": np.asarray(inputs["L0_ffn_sw_up"], np.float32),
        "sw_down": np.asarray(inputs["L0_ffn_sw_down"], np.float32),
    }
    in_maps = []
    for cix in range(8):
        b, j = cix // 4, cix % 4
        m = dict(shared)
        m["x_full"] = x[b]
        m["x_res"] = np.ascontiguousarray(np.concatenate(
            [x[b, 1024 * k + 256 * j:1024 * k + 256 * j + 256, :] for k in range(8)],
            axis=0))
        m["wq"] = np.ascontiguousarray(Wq[:, 512 * j:512 * j + 512])
        m["wk"] = np.ascontiguousarray(Wk[:, 512 * j:512 * j + 512])
        m["wv"] = np.ascontiguousarray(Wv[:, 512 * j:512 * j + 512])
        m["wo_rows"] = np.ascontiguousarray(Wo[512 * j:512 * j + 512, :])
        in_maps.append(m)
    return in_maps


def kernel(**inputs):
    nc = get_nc()
    in_maps = make_in_maps(inputs)
    res = run_bass_kernel_spmd(nc, in_maps, core_ids=list(range(8)))
    out = np.empty((B, S, E), dtype=np.float32)
    for cix in range(8):
        b, j = cix // 4, cix % 4
        o = res.results[cix]["out"]
        for k in range(8):
            out[b, 1024 * k + 256 * j:1024 * k + 256 * j + 256, :] = \
                o[256 * k:256 * k + 256]
    return out
